# version 12
# baseline (speedup 1.0000x reference)
import numpy as np
import ml_dtypes
from contextlib import ExitStack

import concourse.bass as bass
import concourse.mybir as mybir
from concourse.bass_utils import run_bass_kernel_spmd

F32 = mybir.dt.float32
BF16 = mybir.dt.bfloat16
I32 = mybir.dt.int32
AF = mybir.ActivationFunctionType
ALU = mybir.AluOpType
AX = mybir.AxisListType

NCORES = 8
N = 2048
D = 1024
H = 8
NCH = 16
INTER = 4096
EPS = 1e-5
NT = 16
P = 128


class Ctx:

    def __init__(self, nc, stack):
        self.nc, self.stack = nc, stack
        self.h, self.c = {}, {}
        for s in ("pe", "act", "dve", "pool", "ccs"):
            self.h[s] = stack.enter_context(nc.semaphore(s))
            self.c[s] = 0
        self.dh, self.dc = {}, {}

    def inc(self, s, instr, n=1):
        self.c[s] += n
        instr.then_inc(self.h[s], n)
        return self.c[s]

    def w(self, eng, s, val):
        if val and val > 0:
            eng.wait_ge(self.h[s], val)

    def dsem(self, tag):
        if tag not in self.dh:
            self.dh[tag] = self.stack.enter_context(self.nc.semaphore(tag))
            self.dc[tag] = 0
        return self.dh[tag]

    def dma(self, tag, eng, out, in_, **kw):
        sem = self.dsem(tag)
        i = eng.dma_start(out=out, in_=in_, **kw)
        self.dc[tag] += 16
        i.then_inc(sem, 16)
        return self.dc[tag]

    def wd(self, eng, tag, val=None):
        sem = self.dsem(tag)
        v = self.dc[tag] if val is None else val
        if v and v > 0:
            eng.wait_ge(sem, v)


def build(dbg=(), stop=None, skip=()):
    nc = bass.Bass("TRN2", target_bir_lowering=False, detect_race_conditions=False)
    dt = nc.dram_tensor

    x_h = dt("x", [N, D], F32, kind="ExternalInput")
    wq_h = dt("wq", [D, D], F32, kind="ExternalInput")
    wk_h = dt("wk", [D, D], F32, kind="ExternalInput")
    wv_h = dt("wv", [D, D], F32, kind="ExternalInput")
    wo_h = dt("wo", [D, D], F32, kind="ExternalInput")
    wg_h = dt("wg", [D, INTER], F32, kind="ExternalInput")
    wu_h = dt("wu", [D, INTER], F32, kind="ExternalInput")
    wd_h = dt("wd", [INTER, D], F32, kind="ExternalInput")
    w1_h = dt("w1", [INTER, D], F32, kind="ExternalInput")
    w2_h = dt("w2", [D, INTER], F32, kind="ExternalInput")
    anw_h = dt("anw", [P, D], F32, kind="ExternalInput")
    mnw_h = dt("mnw", [P, D], F32, kind="ExternalInput")
    owc_h = dt("owc", [P, 1], F32, kind="ExternalInput")
    mask_h = dt("mask", [P, P], BF16, kind="ExternalInput")
    idf_h = dt("idf", [P, P], F32, kind="ExternalInput")
    onesr_h = dt("onesr", [1, P], BF16, kind="ExternalInput")
    onesf_h = dt("onesf", [P, P], F32, kind="ExternalInput")
    mcol_h = dt("mcol", [P, NCORES], F32, kind="ExternalInput")
    out_h = dt("out", [N, D], F32, kind="ExternalOutput")

    scr_warm = dt("scr_warm", [8, 16], F32, kind="Internal")
    scr_h1 = dt("scr_h1", [N, D], BF16, kind="Internal")
    scr_k = dt("scr_k", [D, N], BF16, kind="Internal")
    scr_sp = dt("scr_sp", [NCH * P, D], BF16, kind="Internal")
    scr_o = dt("scr_o", [N, D], BF16, kind="Internal")
    scr_ao = dt("scr_ao", [D, N], BF16, kind="Internal")
    scr_wo = dt("scr_wo", [D, D], BF16, kind="Internal")
    scr_wq = dt("scr_wq", [D, D], BF16, kind="Internal")
    scr_h2 = dt("scr_h2", [N, D], BF16, kind="Internal")
    scr_mo = dt("scr_mo", [D, N], BF16, kind="Internal")
    scr_z = dt("scr_z", [N, D], BF16, kind="Internal")
    scr_g = dt("scr_g", [INTER, N], BF16, kind="Internal")
    scr_f = dt("scr_f", [D, N], BF16, kind="Internal")
    scr_w1q = dt("scr_w1q", [INTER, D], BF16, kind="Internal")
    scr_w2q = dt("scr_w2q", [D, INTER], BF16, kind="Internal")
    scr_wg = dt("scr_wg", [D, INTER], BF16, kind="Internal")
    scr_wu = dt("scr_wu", [D, INTER], BF16, kind="Internal")
    scr_wd = dt("scr_wd", [INTER, D], BF16, kind="Internal")
    cc_in = dt("cc_in", [P, H * P], F32, kind="Internal")
    cc_out = dt("cc_out", [NCORES * P, H * P], F32, kind="Internal",
                addr_space="Shared")

    DBG = {}
    for name, shape, ddt in (
        ("d_y0", [N, D], F32), ("d_h1", [N, D], BF16), ("d_kfm", [D, N], BF16),
        ("d_ds", [P, H * P], F32), ("d_sst", [P, H * P], F32),
        ("d_o", [N, D], BF16), ("d_ao", [D, N], BF16), ("d_y1", [N, D], F32),
        ("d_h2", [N, D], BF16), ("d_mo", [D, N], BF16), ("d_x1", [N, D], F32),
        ("d_z", [N, D], BF16), ("d_g", [INTER, N], BF16), ("d_f", [D, N], BF16),
        ("d_w1q", [INTER, D], BF16), ("d_w2q", [D, INTER], BF16),
        ("d_scols", [P, 16], F32), ("d_a1c", [P, 176], F32),
    ):
        if name in dbg:
            DBG[name] = dt(name, shape, ddt, kind="ExternalOutput")

    with ExitStack() as stack:
        E = stack.enter_context
        b = Ctx(nc, stack)

        class _LD(dict):
            def __missing__(self, k):
                try:
                    if blk.on:
                        raise KeyError(k)
                except NameError:
                    raise KeyError(k)
                return 0

        LD = _LD()
        wp1_dve_count = {}

        Sf = E(nc.sbuf_tensor("Sf", [P, 16, 1024], F32))
        S2 = E(nc.sbuf_tensor("S2", [P, 8, N], BF16))
        S3 = E(nc.sbuf_tensor("S3", [P, 8, N], BF16))
        S4 = E(nc.sbuf_tensor("S4", [P, 8, N], BF16))
        S5 = E(nc.sbuf_tensor("S5", [P, 16, 1024], BF16))
        mask_sb = E(nc.sbuf_tensor("mask_sb", [P, P], BF16))
        idf_sb = E(nc.sbuf_tensor("idf_sb", [P, P], F32))
        onesr_sb = E(nc.sbuf_tensor("onesr_sb", [1, P], BF16))
        onesf_sb = E(nc.sbuf_tensor("onesf_sb", [P, P], F32))
        owc_sb = E(nc.sbuf_tensor("owc_sb", [P, 1], F32))
        mcol_sb = E(nc.sbuf_tensor("mcol_sb", [P, NCORES], F32))
        stc = E(nc.sbuf_tensor("stc", [P, 176], F32))
        mf0 = E(nc.sbuf_tensor("mf0", [P, 1024], F32)).ap()
        mf1 = E(nc.sbuf_tensor("mf1", [P, 1024], F32)).ap()
        mb0 = E(nc.sbuf_tensor("mb0", [P, 1024], BF16)).ap()
        mb1 = E(nc.sbuf_tensor("mb1", [P, 1024], BF16)).ap()
        trow = E(nc.sbuf_tensor("trow", [16, P], F32)).ap()
        ps = [E(nc.psum_tensor(f"ps{i}", [P, 512], F32)) for i in range(8)]

        class _Blk:
            def __init__(self, blk):
                self.blk, self.on = blk, True

            def _mk(self, n):
                eng = getattr(self.blk, n)
                def deco(fn):
                    if self.on:
                        eng(fn)
                return deco

            @property
            def sync(self): return self._mk("sync")
            @property
            def scalar(self): return self._mk("scalar")
            @property
            def tensor(self): return self._mk("tensor")
            @property
            def vector(self): return self._mk("vector")
            @property
            def gpsimd(self): return self._mk("gpsimd")

        _real_blk = E(nc.Block())
        blk = _Blk(_real_blk)

        def mark(name):
            if stop == name:
                blk.on = False

        _cp = [16]

        def cols(n):
            a = _cp[0]
            _cp[0] += n
            assert _cp[0] <= 176
            return stc[:, a:a + n]

        def col_reset():
            _cp[0] = 16

        c_wsum = stc[:, 0:2]
        c_wtmp = stc[:, 2:4]
        c_wtot = stc[:, 4:6]
        c_th = stc[:, 6:8]
        c_nth = stc[:, 8:10]
        c_s = stc[:, 10:12]
        c_junk = stc[:, 12:13]

        def verify(ve, *aps):
            ve.memset(c_junk, 0.0)
            last = None
            for a in aps:
                r = int(a.shape[0])
                last = ve.tensor_reduce(c_junk[0:r, :], a, axis=AX.X, op=ALU.add)
            return last

        def sfb(lo, hi):
            return Sf[:, lo:hi, :].bitcast(BF16).rearrange("p a d -> p (a d)")

        x_stream = [Sf[:, 0, :], Sf[:, 1, :]]
        wpb = [Sf[:, 2, :], Sf[:, 3, :]]
        anw_sb = Sf[:, 6, :]
        h1_stage = [Sf[:, 7, :], Sf[:, 8, :]]
        wA = sfb(4, 8).rearrange("p (a d) -> p a d", d=1024)
        wB = sfb(8, 12).rearrange("p (a d) -> p a d", d=1024)
        srun = Sf[:, 0, :]
        sstart = Sf[:, 1, :]
        sstart_b = sfb(4, 5)[:, 0:1024]
        sq_buf = sfb(5, 6)
        wq_t1 = sfb(7, 8)
        Gbuf = [Sf[:, 8, :], Sf[:, 9, :]]
        ktm_buf = [sfb(10, 11)[:, 0:1024], sfb(10, 11)[:, 1024:2048]]
        sp_buf = [sfb(11, 12)[:, 0:1024], sfb(11, 12)[:, 1024:2048]]
        spo_buf = [sfb(12, 13)[:, 0:1024], sfb(12, 13)[:, 1024:2048]]
        aT4 = [sfb(13, 14)[:, 0:512].rearrange("p (a d) -> p a d", d=P),
               sfb(13, 14)[:, 512:1024].rearrange("p (a d) -> p a d", d=P)]
        rstd_o = Sf[:, 15, 0:128]
        resid = Sf

        h1_fm = S2
        o_bf = S2[:, :, :].rearrange("p h (c v) -> p h c v", c=NCH)
        h2_fm = S2
        zint_fm = S2
        gq_sb = S2[:, :, :].rearrange("p h (a t) -> p (h a) t", a=4)
        f_tm = S2[:, :, :].rearrange("p h (a d) -> p (h a) d", a=2)

        q_fm = S3
        onrm_fm = S3
        y2acc = S3
        f_fm = S3
        mnw_bf = S3[:, 0, 0:1024]
        junk_b1 = S3[:, 1, 0:1024]

        k_fm = S4
        w1T = S4
        w2T_buf = [S4[:, 0, 0:1024], S4[:, 0, 1024:2048]]
        m2_sb = S4[:, 1, :]
        junk_c = S4[:, 2, 0:1024]

        def wg_view(par):
            return S4[:, par * 4:par * 4 + 2, :].rearrange("p a t -> p (a t)").rearrange(
                "p (a o) -> p a o", o=512)

        def wu_view(par):
            return S4[:, par * 4 + 2:par * 4 + 4, :].rearrange("p a t -> p (a t)").rearrange(
                "p (a o) -> p a o", o=512)

        v_tm = S5[:, :, :]
        wo_sb = S5[:, 0:8, :]
        flipb = [S5[:, 8, :], S5[:, 9, :]]
        wd_view = [S5[:, 0:4, :], S5[:, 4:8, :]]
        sg_bufs = [S5[:, 10, 0:512], S5[:, 10, 512:1024]]
        g_tile = S5[:, 11:13, :].rearrange("p a (c t) -> p (a c) t", c=2)
        m1_sb = S5[:, 0:2, :].rearrange("p a d -> p (a d)")
        gmax = S5[:, 2:6, :].bitcast(F32).rearrange("p a d -> p (a d)")
        g2row = [S5[:, 6:8, :].rearrange("p a d -> p (a d)"),
                 S5[:, 8:10, :].rearrange("p a d -> p (a d)")]
        mrow_f = S5[:, 10:14, :].bitcast(F32).rearrange("p a d -> p (a d)")
        gmin = S5[:, 10:14, :].bitcast(F32).rearrange("p a d -> p (a d)")
        mrow_b = S5[:, 14:16, :].rearrange("p a d -> p (a d)")
        g2rb = [S5[:, 6, 0:512], S5[:, 6, 512:1024]]
        fstage = [S5[:, 8, :], S5[:, 9, :]]

        def wblk(src, i5):
            if i5 < 32:
                return src[i5 * P:(i5 + 1) * P, :]
            i6 = i5 - 32
            rb, cb = i6 // 4, i6 % 4
            return src[rb * P:(rb + 1) * P, cb * 1024:(cb + 1) * 1024]

        ewp = [Sf[:, 14, :], Sf[:, 15, :]]

        @blk.sync
        def _(sy):
            for ddst, dsrc in ((mask_sb[:, :], mask_h[:, :]), (idf_sb[:, :], idf_h[:, :]),
                               (onesr_sb[:, :], onesr_h[:, :]), (onesf_sb[:, :], onesf_h[:, :]),
                               (owc_sb[:, :], owc_h[:, :]), (mcol_sb[:, :], mcol_h[:, :]),
                               (anw_sb, anw_h[:, :])):
                c = b.dma("dCONST", sy, ddst, dsrc)
            LD["consts"] = ("dCONST", c)
            b.dma("dWPE", sy, wpb[0], wblk(w1_h, 0))
            b.dma("dWPE", sy, wpb[1], wblk(w1_h, 1))
            b.dma("dWPE", sy, ewp[0], wblk(w2_h, 32))
            b.dma("dWPE", sy, ewp[1], wblk(w2_h, 33))

        @blk.scalar
        def _(ac):
            for k in range(3):
                b.dma("dWRM", ac, scr_warm[k:k + 1, :], stc[0:1, 0:16])

        @blk.sync
        def _(sy):
            for k in range(3, 6):
                b.dma("dWRMS", sy, scr_warm[k:k + 1, :], stc[0:1, 0:16])

        c_sum = cols(16); c_ssq = cols(16); c_m = cols(16); c_var = cols(16)
        c_rho1 = cols(16); c_rho = cols(16); c_nmr1 = cols(16); c_nmr = cols(16)
        c_tmp = cols(16)

        a1_dve0 = b.c["dve"]

        @blk.sync
        def _(sy):
            for t in range(NT):
                b.w(sy, "dve", a1_dve0 + (t - 2) + 1 if t >= 2 else 0)
                LD[f"a1x{t}"] = (f"dA1X{t % 2}", b.dma(f"dA1X{t % 2}", sy, x_stream[t % 2],
                                                       x_h[t * P:(t + 1) * P, :]))

        @blk.vector
        def _(ve):
            for t in range(NT):
                b.wd(ve, *LD[f"a1x{t}"])
                ve.reduce_sum(c_sum[:, t:t + 1], x_stream[t % 2], axis=AX.X)
                i = ve.scalar_tensor_tensor(mb0[:, :], x_stream[t % 2], 1.0, x_stream[t % 2],
                                            ALU.mult, ALU.mult, accum_out=c_ssq[:, t:t + 1])
                b.inc("dve", i)
            ve.tensor_scalar(c_m, c_sum, 1.0 / D, None, ALU.mult)
            ve.tensor_scalar(c_tmp, c_ssq, 1.0 / D, None, ALU.mult)
            ve.tensor_mul(c_var, c_m, c_m)
            ve.memset(c_junk, 0.0)
            ve.tensor_sub(c_var, c_tmp, c_var)
            ve.memset(c_junk, 0.0)
            ve.tensor_scalar(c_tmp, c_var, EPS, None, ALU.add)
            LD["a1vpe"] = b.inc("dve", verify(ve, c_tmp))

        @blk.scalar
        def _(ac):
            b.w(ac, "dve", LD["a1vpe"])
            ac.activation(c_rho1, c_tmp, AF.Ln)
            i = ac.activation(c_rho1, c_rho1, AF.Exp, scale=-0.5)
            LD["a1rho1"] = b.inc("act", i)

        @blk.vector
        def _(ve):
            b.w(ve, "act", LD["a1rho1"])
            i = ve.reciprocal(c_rho, c_tmp)
            b.inc("dve", i)
            b.w(ve, "dve", b.c["dve"])
            ve.tensor_mul(c_rho, c_var, c_rho)
            ve.memset(c_junk, 0.0)
            ve.tensor_scalar(c_rho, c_rho, EPS, None, ALU.add)
            LD["a1r2"] = b.inc("dve", verify(ve, c_rho))

        @blk.scalar
        def _(ac):
            b.w(ac, "dve", LD["a1r2"])
            ac.activation(c_rho, c_rho, AF.Ln)
            i = ac.activation(c_rho, c_rho, AF.Exp, scale=-0.5)
            LD["a1rho2"] = b.inc("act", i)

        @blk.vector
        def _(ve):
            b.w(ve, "act", LD["a1rho2"])
            ve.tensor_mul(c_rho, c_rho1, c_rho)
            ve.scalar_tensor_tensor(c_nmr1, c_m, -1.0, c_rho1, ALU.mult, ALU.mult)
            ve.scalar_tensor_tensor(c_nmr, c_m, -1.0, c_rho, ALU.mult, ALU.mult)
            LD["a1cols"] = b.inc("dve", verify(ve, c_rho, c_nmr1, c_nmr))

        if "d_a1c" in DBG:
            @blk.sync
            def _(sy):
                b.w(sy, "dve", LD["a1cols"])
                b.dma("dDBG", sy, DBG["d_a1c"][:, :], stc[:, :])

        a1_act0 = b.c["act"]
        a1_hv0 = b.c["dve"]

        @blk.sync
        def _(sy):
            for t in range(NT):
                if t >= 2:
                    b.w(sy, "act", a1_act0 + (t - 2) + 1)
                else:
                    b.w(sy, "dve", a1_dve0 + 15 + t)
                LD[f"a1x2{t}"] = (f"dA1Y{t % 2}", b.dma(f"dA1Y{t % 2}", sy, x_stream[t % 2],
                                                        x_h[t * P:(t + 1) * P, :]))

        @blk.scalar
        def _(ac):
            b.w(ac, "dve", LD["a1cols"])
            for t in range(NT):
                b.wd(ac, *LD[f"a1x2{t}"])
                if t >= 2:
                    b.w(ac, "dve", a1_hv0 + (t - 2) + 1)
                else:
                    ac.copy(mb0 if t == 0 else mb1, x_stream[t % 2])
                b.inc("act", ac.activation(h1_stage[t % 2], x_stream[t % 2], AF.Identity,
                                           scale=c_rho[:, t:t + 1], bias=c_nmr[:, t:t + 1]))
                if t >= 1:
                    b.w(ac, "dve", a1_hv0 + (t - 1) + 1)
                    src = mb0[:, :] if (t - 1) % 2 == 0 else mb1[:, :]
                    LD[f"a1h1d{t - 1}"] = (f"dH1{(t - 1) % 2}",
                                           b.dma(f"dH1{(t - 1) % 2}", ac,
                                                 scr_h1[(t - 1) * P:t * P, :], src))
            b.w(ac, "dve", a1_hv0 + NT)
            b.dma("dH11", ac, scr_h1[(NT - 1) * P:NT * P, :], mb1[:, :])
            LD[f"a1h1d{NT - 1}"] = ("dH11", b.dma("dH11", ac,
                                                  scr_h1[(NT - 1) * P:NT * P, :], mb1[:, :]))

        @blk.vector
        def _(ve):
            b.wd(ve, *LD["consts"])
            for t in range(NT):
                b.w(ve, "act", a1_act0 + t + 1)
                if t >= 2:
                    b.wd(ve, *LD[f"a1h1d{t - 2}"])
                dst = mb0[:, :] if t % 2 == 0 else mb1[:, :]
                i = ve.tensor_mul(dst, h1_stage[t % 2], anw_sb)
                b.inc("dve", i)
        LD["a1_dve"] = b.c["dve"]
        LD["a1_act"] = b.c["act"]

        @blk.gpsimd
        def _(gp):
            b.w(gp, "dve", LD["a1_dve"])
            b.w(gp, "act", LD["a1_act"])
            LD["wk"] = ("dWK", b.dma("dWK", gp, wA, wk_h.ap().rearrange("(a p) o -> p a o", p=P)))
            LD["wv"] = ("dWV", b.dma("dWV", gp, wB, wv_h.ap().rearrange("(a p) o -> p a o", p=P)))
            LD["wo_pre"] = ("dWOP", b.dma("dWOP", gp, scr_wo[:, :], wo_h[:, :]))
            LD["wq_pre"] = ("dWQP", b.dma("dWQP", gp, scr_wq[:, :], wq_h[:, :]))

        if "d_h1" in DBG:
            @blk.sync
            def _(sy):
                b.wd(sy, "dH10"); b.wd(sy, "dH11")
                b.dma("dDBG", sy, DBG["d_h1"][:, :], scr_h1[:, :])

        @blk.sync
        def _(sy):
            b.wd(sy, "dH10"); b.wd(sy, "dH11")
            for rep in range(2):
                for fc in range(8):
                    c = b.dma("dH1F", sy, h1_fm[:, fc, :],
                              scr_h1[:, fc * P:(fc + 1) * P], transpose=True)
            LD["h1fm"] = ("dH1F", c)

        mark("a2")
        a3a_act0 = b.c["act"]

        @blk.tensor
        def _(te):
            if "a3a" in skip:
                return
            b.wd(te, *LD["wk"])
            b.wd(te, *LD["wv"])
            b.wd(te, *LD["h1fm"])
            te.matmul(ps[0][:, :], wA[:, 7, 0:P], h1_fm[:, 7, 0:512],
                      start=True, stop=True)
            te.matmul(ps[1][:, :], h1_fm[:, 7, 0:P], wB[:, 7, 0:512],
                      start=True, stop=True)
            r = 0
            for j in range(64):
                if j >= 4:
                    b.w(te, "act", a3a_act0 + (j - 4) + 1)
                if j < 32:
                    h, nb = j // 4, j % 4
                    for kc in range(8):
                        i = te.matmul(ps[r][:, :], wA[:, kc, h * P:(h + 1) * P],
                                      h1_fm[:, kc, nb * 512:(nb + 1) * 512],
                                      start=(kc == 0), stop=(kc == 7))
                else:
                    tc, hf = (j - 32) // 2, (j - 32) % 2
                    for kc in range(8):
                        i = te.matmul(ps[r][:, :], h1_fm[:, kc, tc * P:(tc + 1) * P],
                                      wB[:, kc, hf * 512:(hf + 1) * 512],
                                      start=(kc == 0), stop=(kc == 7))
                b.inc("pe", i)
                LD[f"a3m{j}"] = b.c["pe"]
                r = (r + 1) % 4
        LD["a3a_pe"] = b.c["pe"]

        @blk.scalar
        def _(ac):
            if "a3a" in skip:
                return
            r = 0
            for j in range(64):
                b.w(ac, "pe", LD[f"a3m{j}"])
                if j < 32:
                    h, nb = j // 4, j % 4
                    i = ac.copy(k_fm[:, h, nb * 512:(nb + 1) * 512], ps[r][:, :])
                else:
                    tc, hf = (j - 32) // 2, (j - 32) % 2
                    i = ac.copy(v_tm[:, tc, hf * 512:(hf + 1) * 512], ps[r][:, :])
                b.inc("act", i)
                r = (r + 1) % 4
            LD["a3a_act"] = b.c["act"]
            b.dma("dSCRK", ac, scr_k.ap().rearrange("(h p) t -> p h t", p=P), k_fm[:, :, :])
            LD["scrk"] = ("dSCRK", b.dma("dSCRK", ac, scr_k.ap().rearrange("(h p) t -> p h t", p=P),
                                         k_fm[:, :, :]))
        bank_last = {i: None for i in range(8)}
        if "a3a" not in skip:
            for i in range(4):
                bank_last[i] = ("act", LD["a3a_act"])

        a4_pe0 = b.c["pe"]
        a4_dve1 = None

        @blk.sync
        def _(sy):
            if "a4" in skip:
                return
            b.wd(sy, *LD["scrk"])
            for c in range(NCH):
                b.w(sy, "pe", a4_pe0 + (c - 2) + 1 if c >= 2 else 0)
                if c < 2:
                    b.dma(f"dKT{c % 2}", sy, ktm_buf[c % 2],
                          scr_k[:, c * P:(c + 1) * P], transpose=True)
                LD[f"a4k{c}"] = (f"dKT{c % 2}", b.dma(f"dKT{c % 2}", sy, ktm_buf[c % 2],
                                                      scr_k[:, c * P:(c + 1) * P], transpose=True))

        @blk.vector
        def _(ve):
            if "a4" in skip:
                return
            b.w(ve, "act", LD["a1_act"])
            i = ve.memset(srun, 0.0)
            LD["srun0"] = b.inc("dve", i)

        def wp1_wait(i5):
            if i5 % 2 == 0:
                n = i5 // 2 if i5 <= 30 else i5 // 2 - 1
                return ("dWP10", 16 * n)
            n = (i5 - 1) // 2 if i5 <= 31 else (i5 - 1) // 2 - 1
            return ("dWP11", 16 * n)

        @blk.vector
        def _(ve):
            if "wprep" in skip:
                return
            for i5 in range(32):
                if i5 < 2:
                    b.wd(ve, "dWPE", 64)
                else:
                    b.wd(ve, *wp1_wait(i5))
                dst = c_wsum[:, 0:1] if i5 == 0 else c_wtmp[:, 0:1]
                b.inc("dve", ve.tensor_reduce(dst, wpb[i5 % 2], axis=AX.X, op=ALU.add,
                                              apply_absolute_value=True))
                ve.memset(c_junk, 0.0)
                ve.tensor_reduce(dst, wpb[i5 % 2], axis=AX.X, op=ALU.add,
                                 apply_absolute_value=True)
                if i5 > 0:
                    ve.memset(c_junk, 0.0)
                    b.inc("dve", ve.tensor_add(c_wsum[:, 0:1], c_wsum[:, 0:1], c_wtmp[:, 0:1]))
                wp1_dve_count[i5] = b.c["dve"]
        a4_dve1 = b.c["dve"]

        @blk.tensor
        def _(te):
            if "a4" in skip:
                return
            b.w(te, "dve", LD["srun0"])
            for c in range(NCH):
                b.wd(te, *LD[f"a4k{c}"])
                if c == 0:
                    b.wd(te, *LD["a4k1"])
                b.w(te, "dve", a4_dve1 + (c - 2) + 1 if c >= 2 else 0)
                pr = (ps[4], ps[5]) if c % 2 == 0 else (ps[6], ps[7])
                for h in range(H):
                    i = te.matmul(pr[h // 4][:, (h % 4) * P:(h % 4 + 1) * P],
                                  ktm_buf[c % 2][:, h * P:(h + 1) * P],
                                  v_tm[:, c, h * P:(h + 1) * P], start=True, stop=True)
                b.inc("pe", i)
                LD[f"a4m{c}"] = b.c["pe"]

        a4_act0 = b.c["act"]

        @blk.vector
        def _(ve):
            if "a4" in skip:
                return
            for c in range(NCH):
                b.w(ve, "pe", LD[f"a4m{c}"])
                b.w(ve, "act", (a4_act0 + (2 if c == 1 else c + 1)) if c >= 1 else 0)
                pr = (ps[4], ps[5]) if c % 2 == 0 else (ps[6], ps[7])
                ve.tensor_add(srun[:, 0:512], srun[:, 0:512], pr[0][:, :])
                i = ve.tensor_add(srun[:, 512:1024], srun[:, 512:1024], pr[1][:, :])
                b.inc("dve", i)
                LD[f"a4s{c}"] = b.c["dve"]
        LD["a4_dve"] = b.c["dve"]

        @blk.vector
        def _(ve):
            if "wprep" in skip:
                return
            for i5 in range(32, 64):
                if i5 < 34:
                    b.wd(ve, "dWPE", 64)
                    src = ewp[i5 - 32]
                else:
                    b.wd(ve, *wp1_wait(i5))
                    src = wpb[i5 % 2]
                dst = c_wsum[:, 1:2] if i5 == 32 else c_wtmp[:, 1:2]
                b.inc("dve", ve.tensor_reduce(dst, src, axis=AX.X, op=ALU.add,
                                              apply_absolute_value=True))
                ve.memset(c_junk, 0.0)
                ve.tensor_reduce(dst, src, axis=AX.X, op=ALU.add,
                                 apply_absolute_value=True)
                if i5 > 32:
                    ve.memset(c_junk, 0.0)
                    b.inc("dve", ve.tensor_add(c_wsum[:, 1:2], c_wsum[:, 1:2], c_wtmp[:, 1:2]))
                wp1_dve_count[i5] = b.c["dve"]
            ve.memset(c_junk, 0.0)
            LD["wsum2"] = b.inc("dve", verify(ve, c_wsum))
        LD["wsum"] = b.c["dve"]

        @blk.scalar
        def _(ac):
            if "a4" in skip:
                return
            for c in range(NCH):
                b.w(ac, "dve", LD[f"a4s{c}"])
                if c >= 2:
                    b.wd(ac, *LD[f"a4w{c - 2}"])
                b.inc("act", ac.copy(spo_buf[c % 2], srun))
                if c == 0:
                    LD["spb1"] = b.inc("act", ac.copy(sp_buf[1], srun))
                if c >= 1:
                    cp = c - 1
                    LD[f"a4w{cp}"] = (f"dSP{cp % 2}",
                                      b.dma(f"dSP{cp % 2}", ac,
                                            scr_sp[cp * P:(cp + 1) * P, :], spo_buf[cp % 2]))
            ac.copy(sstart[:, 0:512], srun[:, 0:512])
            cp = NCH - 1
            LD[f"a4w{cp}"] = (f"dSP{cp % 2}",
                              b.dma(f"dSP{cp % 2}", ac,
                                    scr_sp[cp * P:(cp + 1) * P, :], spo_buf[cp % 2]))
            LD["ccin"] = ("dCCIN", b.dma("dCCIN", ac, cc_in[:, :], srun))
        for i in range(4, 8):
            bank_last[i] = ("dve", LD["a4_dve"])


        @blk.gpsimd
        def _(gp):
            if "wprep" not in skip:
                for i5 in range(64):
                    if i5 in (0, 1, 32, 33):
                        continue
                    src = w1_h if i5 < 32 else w2_h
                    gate = i5 - 4 if i5 in (34, 35) else i5 - 2
                    b.w(gp, "dve", wp1_dve_count[gate])
                    LD[f"wp1l{i5}"] = (f"dWP1{i5 % 2}", b.dma(f"dWP1{i5 % 2}", gp,
                                                              wpb[i5 % 2], wblk(src, i5)))
            if "cc" not in skip and "a4" not in skip:
                b.wd(gp, *LD["ccin"])
                i = gp.collective_compute("AllGather", ALU.bypass,
                                          replica_groups=[list(range(NCORES))],
                                          ins=[cc_in.ap().opt()], outs=[cc_out.ap().opt()])
                LD["cc"] = b.inc("ccs", i)
            pass

        @blk.sync
        def _(sy):
            if "a3b" in skip:
                return
            b.wd(sy, *LD["wq_pre"])
            b.w(sy, "pe", LD["a3a_pe"] if "a3a" not in skip else 0)
            LD["wqsb"] = ("dWQS", b.dma("dWQS", sy, wA,
                                        scr_wq.ap().rearrange("(a p) o -> p a o", p=P)))

        a3b_act0 = b.c["act"]

        @blk.tensor
        def _(te):
            if "a3b" in skip:
                return
            b.wd(te, *LD["wqsb"])
            te.matmul(ps[0][:, :], wA[:, 7, 0:P], h1_fm[:, 7, 0:512],
                      start=True, stop=True)
            r = 0
            for j in range(32):
                h, nb = j // 4, j % 4
                if j >= 4:
                    b.w(te, "act", a3b_act0 + (j - 4) + 1)
                for kc in range(8):
                    i = te.matmul(ps[r][:, :], wA[:, kc, h * P:(h + 1) * P],
                                  h1_fm[:, kc, nb * 512:(nb + 1) * 512],
                                  start=(kc == 0), stop=(kc == 7))
                b.inc("pe", i)
                LD[f"a3q{j}"] = b.c["pe"]
                r = (r + 1) % 4
        LD["a3b_pe"] = b.c["pe"]

        @blk.scalar
        def _(ac):
            if "a3b" in skip:
                return
            r = 0
            for j in range(32):
                h, nb = j // 4, j % 4
                b.w(ac, "pe", LD[f"a3q{j}"])
                b.inc("act", ac.copy(q_fm[:, h, nb * 512:(nb + 1) * 512], ps[r][:, :]))
                r = (r + 1) % 4
            LD["a3b_act"] = b.c["act"]
        if "a3b" not in skip:
            for i in range(4):
                bank_last[i] = ("act", LD["a3b_act"])

        @blk.tensor
        def _(te):
            if "wprep" in skip:
                return
            b.w(te, "dve", LD["wsum"])
            b.w(te, bank_last[4][0], bank_last[4][1])
            i = te.matmul(ps[4][:, 0:2], onesf_sb[:, :], c_wsum, start=True, stop=True)
            b.inc("pe", i)
            LD["wtotmm"] = b.c["pe"]

        @blk.vector
        def _(ve):
            if "wprep" in skip:
                return
            b.w(ve, "pe", LD["wtotmm"])
            i = ve.tensor_copy(c_wtot, ps[4][:, 0:2])
            b.inc("dve", i)
            b.w(ve, "dve", b.c["dve"])
            b.inc("dve", ve.tensor_scalar(c_s, c_wtot, 1.0 / (INTER * D), None, ALU.mult))
            b.w(ve, "dve", b.c["dve"])
            b.inc("dve", ve.tensor_scalar(c_s, c_s, EPS, None, ALU.max))
            b.w(ve, "dve", b.c["dve"])
            b.inc("dve", ve.tensor_scalar(c_th, c_s, 0.5, None, ALU.mult))
            i = ve.tensor_scalar(c_nth, c_s, -0.5, None, ALU.mult)
            LD["wth"] = b.inc("dve", i)
        if "wprep" not in skip:
            bank_last[4] = ("dve", LD["wth"])

        if "d_scols" in DBG:
            @blk.sync
            def _(sy):
                b.w(sy, "dve", LD["wth"])
                b.dma("dDBG", sy, DBG["d_scols"][:, :], stc[:, 0:16])
        if "d_kfm" in DBG:
            @blk.sync
            def _(sy):
                b.wd(sy, *LD["scrk"])
                b.dma("dDBG", sy, DBG["d_kfm"][:, :], scr_k[:, :])
        if "d_ds" in DBG:
            @blk.sync
            def _(sy):
                b.wd(sy, *LD["ccin"])
                b.dma("dDBG", sy, DBG["d_ds"][:, :], cc_in[:, :])

        mark("pre_a6")
        a6_pe0 = b.c["pe"]
        a6_dve0 = b.c["dve"]
        a6_act0 = b.c["act"]

        @blk.sync
        def _(sy):
            for c in range(2, NCH):
                b.w(sy, "pe", a6_pe0 + 3 * (c - 2) + 3 if c >= 3 else 0)
                b.wd(sy, *LD[f"a4w{c - 1}"])
                b.wd(sy, *LD[f"a4w{c}"])
                LD[f"a6p{c}"] = (f"dA6P{c % 2}", b.dma(f"dA6P{c % 2}", sy, sp_buf[c % 2],
                                                       scr_sp[(c - 1) * P:c * P, :]))

        @blk.tensor
        def _(te):
            b.w(te, "act", LD["a3b_act"])
            for c in range(NCH):
                if c == 1:
                    b.w(te, "act", LD["spb1"])
                elif c >= 2:
                    b.wd(te, *LD[f"a6p{c}"])
                for half in range(2):
                    b.w(te, "dve", a6_dve0 + 2 * (c - 1) + half + 1 if c >= 1 else 0)
                    if c == 0 and bank_last[6 + half]:
                        b.w(te, bank_last[6 + half][0], bank_last[6 + half][1])
                    for hh in range(4):
                        h = half * 4 + hh
                        i = te.matmul(ps[6 + half][:, hh * P:(hh + 1) * P],
                                      k_fm[:, h, c * P:(c + 1) * P],
                                      q_fm[:, h, c * P:(c + 1) * P], start=True, stop=True)
                    b.inc("pe", i)
                b.w(te, "act", a6_act0 + 2 * (c - 2) + 2 if c >= 2 else 0)
                if c < 2:
                    for bb in (c % 2, 2 + c % 2):
                        if bank_last[bb]:
                            b.w(te, bank_last[bb][0], bank_last[bb][1])
                b.w(te, "dve", a6_dve0 + 2 * c + 2)
                for half in range(2):
                    ob = ps[half * 2 + (c % 2)]
                    for hh in range(4):
                        h = half * 4 + hh
                        i = te.matmul(ob[:, hh * P:(hh + 1) * P], aT4[half][:, hh, :],
                                      v_tm[:, c, h * P:(h + 1) * P],
                                      start=True, stop=(c == 0))
                        if c >= 1:
                            i = te.matmul(ob[:, hh * P:(hh + 1) * P],
                                          q_fm[:, h, c * P:(c + 1) * P],
                                          sp_buf[c % 2][:, h * P:(h + 1) * P],
                                          start=False, stop=True)
                b.inc("pe", i)
                LD[f"a6o{c}"] = b.c["pe"]
        LD["a6_pe"] = b.c["pe"]

        @blk.vector
        def _(ve):
            for c in range(NCH):
                for half in range(2):
                    b.w(ve, "pe", a6_pe0 + 3 * c + half + 1)
                    i = ve.tensor_tensor(aT4[half][:, :, :],
                                         ps[6 + half][:, :].rearrange("p (a d) -> p a d", d=P),
                                         mask_sb[:, :].unsqueeze(1).broadcast_to([P, 4, P]),
                                         ALU.mult)
                    b.inc("dve", i)
        LD["a6_dve"] = b.c["dve"]

        @blk.scalar
        def _(ac):
            for c in range(NCH):
                b.w(ac, "pe", LD[f"a6o{c}"])
                for half in range(2):
                    ob = ps[half * 2 + (c % 2)]
                    i = ac.copy(o_bf[:, half * 4:(half + 1) * 4, c, :],
                                ob[:, :].rearrange("p (a d) -> p a d", d=P))
                    b.inc("act", i)
            LD["a6_act"] = b.c["act"]
        for i in range(4):
            bank_last[i] = ("act", LD["a6_act"])
        bank_last[6] = ("dve", LD["a6_dve"])
        bank_last[7] = ("dve", LD["a6_dve"])

        a7_dve0 = b.c["dve"]

        @blk.sync
        def _(sy):
            b.w(sy, "ccs", LD["cc"])
            for r_ in range(NCORES):
                b.w(sy, "dve", a7_dve0 + (r_ - 2) + 2 if r_ >= 2 else 0)
                LD[f"a7l{r_}"] = (f"dG{r_ % 2}", b.dma(f"dG{r_ % 2}", sy, Gbuf[r_ % 2],
                                                       cc_out[r_ * P:(r_ + 1) * P, :]))

        @blk.vector
        def _(ve):
            b.w(ve, "pe", LD["a3b_pe"])
            i = ve.memset(sstart, 0.0)
            b.inc("dve", i)
            for r_ in range(NCORES):
                b.wd(ve, *LD[f"a7l{r_}"])
                ve.tensor_reduce(c_junk, Gbuf[r_ % 2], axis=AX.X, op=ALU.add)
                ve.memset(c_junk, 0.0)
                i = ve.scalar_tensor_tensor(sstart, Gbuf[r_ % 2], mcol_sb[:, r_:r_ + 1],
                                            sstart, ALU.mult, ALU.add)
                b.inc("dve", i)
            LD["a7ss"] = b.c["dve"]

        @blk.scalar
        def _(ac):
            b.w(ac, "dve", LD["a7ss"])
            b.w(ac, "pe", LD["a3b_pe"])
            i = ac.copy(sstart_b, sstart)
            LD["a7ssb"] = b.inc("act", i)

        if "d_sst" in DBG:
            @blk.sync
            def _(sy):
                b.w(sy, "dve", LD["a7ss"])
                b.dma("dDBG", sy, DBG["d_sst"][:, :], sstart)

        a7_add0 = b.c["dve"]

        @blk.tensor
        def _(te):
            b.w(te, "act", LD["a7ssb"])
            b.w(te, "act", LD["a6_act"])
            for j in range(32):
                h, cg = j // 4, j % 4
                bk = ps[4 + j % 2]
                if j >= 2:
                    b.w(te, "dve", a7_add0 + (j - 2) + 1)
                elif bank_last[4 + j % 2]:
                    b.w(te, bank_last[4 + j % 2][0], bank_last[4 + j % 2][1])
                for cc2 in range(4):
                    c = cg * 4 + cc2
                    i = te.matmul(bk[:, cc2 * P:(cc2 + 1) * P],
                                  q_fm[:, h, c * P:(c + 1) * P],
                                  sstart_b[:, h * P:(h + 1) * P], start=True, stop=True)
                b.inc("pe", i)
                LD[f"a7m{j}"] = b.c["pe"]
        LD["a7_pe"] = b.c["pe"]

        @blk.vector
        def _(ve):
            for j in range(32):
                h, cg = j // 4, j % 4
                b.w(ve, "pe", LD[f"a7m{j}"])
                i = ve.tensor_tensor(o_bf[:, h, cg * 4:(cg + 1) * 4, :],
                                     o_bf[:, h, cg * 4:(cg + 1) * 4, :],
                                     ps[4 + j % 2][:, :].rearrange("p (a d) -> p a d", d=P),
                                     ALU.add)
                b.inc("dve", i)
            LD["a7add"] = b.c["dve"]
        bank_last[4] = ("dve", LD["a7add"])
        bank_last[5] = ("dve", LD["a7add"])

        a7_rd0 = b.c["dve"]

        @blk.scalar
        def _(ac):
            b.w(ac, "dve", LD["a7add"])
            for h in range(H):
                b.w(ac, "dve", a7_rd0 + (h - 1) + 1 if h >= 1 else 0)
                i = ac.activation(sq_buf[:, :], S2[:, h, :], AF.Square)
                b.inc("act", i)
                LD[f"a7sq{h}"] = b.c["act"]

        @blk.vector
        def _(ve):
            for h in range(H):
                b.w(ve, "act", LD[f"a7sq{h}"])
                i = ve.tensor_reduce(rstd_o[:, h * 16:(h + 1) * 16].unsqueeze(2),
                                     sq_buf[:, :].rearrange("p (a d) -> p a d", d=P),
                                     axis=AX.X, op=ALU.add)
                b.inc("dve", i)
            ve.memset(c_junk, 0.0)
            ve.tensor_scalar(rstd_o, rstd_o, 1.0 / P, EPS, ALU.mult, ALU.add)
            LD["a7ms"] = b.inc("dve", verify(ve, rstd_o))

        @blk.scalar
        def _(ac):
            b.w(ac, "dve", LD["a7ms"])
            ac.activation(rstd_o, rstd_o, AF.Ln)
            i = ac.activation(rstd_o, rstd_o, AF.Exp, scale=-0.5)
            LD["a7rstd"] = b.inc("act", i)

        @blk.vector
        def _(ve):
            b.w(ve, "act", LD["a7rstd"])
            for h in range(H):
                i = ve.tensor_tensor(o_bf[:, h, :, :], o_bf[:, h, :, :],
                                     rstd_o[:, h * 16:(h + 1) * 16].unsqueeze(2).broadcast_to([P, 16, P]),
                                     ALU.mult)
                b.inc("dve", i)
            LD["a7on"] = b.c["dve"]

        @blk.scalar
        def _(ac):
            b.w(ac, "dve", LD["a7on"])
            if "d_o" in DBG:
                b.dma("dSCRO", ac, scr_o.ap().rearrange("(c p) (h v) -> p h c v", p=P, h=H),
                      o_bf[:, :, :, :])
                LD["scro"] = ("dSCRO", b.dma("dSCRO", ac,
                                             scr_o.ap().rearrange("(c p) (h v) -> p h c v", p=P, h=H),
                                             o_bf[:, :, :, :]))

        if "d_o" in DBG:
            @blk.sync
            def _(sy):
                b.wd(sy, *LD["scro"])
                b.dma("dDBG", sy, DBG["d_o"][:, :], scr_o[:, :])

        mark("a7")
        @blk.sync
        def _(sy):
            b.w(sy, "dve", LD["a7on"])
            b.w(sy, "pe", LD["a7_pe"])
            for h in range(H):
                for cch in range(NCH):
                    c = b.dma("dONRM", sy, onrm_fm[:, h, cch * P:(cch + 1) * P],
                              o_bf[:, h, cch, :], transpose=True)
            LD["onrm"] = ("dONRM", c)
            b.wd(sy, *LD["wo_pre"])
            LD["wosb"] = ("dWOS", b.dma("dWOS", sy, wo_sb[:, :, :],
                                        scr_wo.ap().rearrange("(a p) o -> p a o", p=P)))

        @blk.vector
        def _(ve):
            b.wd(ve, *LD["wosb"])
            ve.tensor_reduce(c_junk, wo_sb[:, 7, :], axis=AX.X, op=ALU.add)
            ve.memset(c_junk, 0.0)
            for kc in range(8):
                i = ve.tensor_scalar(wo_sb[:, kc, :], wo_sb[:, kc, :], owc_sb[:, 0:1], None, ALU.mult)
                b.inc("dve", i)
            LD["wofold"] = b.c["dve"]

        a9_act0 = b.c["act"]

        @blk.tensor
        def _(te):
            b.w(te, "dve", LD["wofold"])
            b.wd(te, *LD["onrm"])
            if bank_last[0]:
                b.w(te, bank_last[0][0], bank_last[0][1])
            te.matmul(ps[0][:, :], wo_sb[:, 7, 0:P], onrm_fm[:, 7, 0:512],
                      start=True, stop=True)
            r = 0
            for j in range(32):
                oc, nb = j // 4, j % 4
                if j >= 4:
                    b.w(te, "act", a9_act0 + (j - 4) + 1)
                elif bank_last[r]:
                    b.w(te, bank_last[r][0], bank_last[r][1])
                for hc in range(8):
                    i = te.matmul(ps[r][:, :], wo_sb[:, hc, oc * P:(oc + 1) * P],
                                  onrm_fm[:, hc, nb * 512:(nb + 1) * 512],
                                  start=(hc == 0), stop=(hc == 7))
                b.inc("pe", i)
                LD[f"a9m{j}"] = b.c["pe"]
                r = (r + 1) % 4
        LD["a9_pe"] = b.c["pe"]

        @blk.scalar
        def _(ac):
            r = 0
            aobuf = [mb0[:, 0:512], mb1[:, 0:512], mb0[:, 512:1024], mb1[:, 512:1024]]

            def ao_store(jj):
                oc_, nb_ = jj // 4, jj % 4
                src = aobuf[jj % 4]
                b.dma(f"dAO{jj % 2}", ac,
                      scr_ao[oc_ * P:(oc_ + 1) * P, nb_ * 512:(nb_ + 1) * 512], src)
                LD[f"a9w{jj}"] = (f"dAO{jj % 2}",
                                  b.dma(f"dAO{jj % 2}", ac,
                                        scr_ao[oc_ * P:(oc_ + 1) * P,
                                               nb_ * 512:(nb_ + 1) * 512], src))
            for j in range(32):
                b.w(ac, "pe", LD[f"a9m{j}"])
                if j >= 4:
                    b.wd(ac, *LD[f"a9w{j - 4}"])
                b.inc("act", ac.copy(aobuf[j % 4], ps[r][:, :]))
                if j >= 1:
                    ao_store(j - 1)
                r = (r + 1) % 4
            ac.copy(mf0[:, 0:256], ps[0][:, 0:256])
            ao_store(31)
            LD["a9_act"] = b.c["act"]
        for i in range(4):
            bank_last[i] = ("act", LD["a9_act"])

        if "d_ao" in DBG:
            @blk.sync
            def _(sy):
                b.wd(sy, "dAO0"); b.wd(sy, "dAO1")
                b.dma("dDBG", sy, DBG["d_ao"][:, :], scr_ao[:, :])

        a9_y10 = b.c["dve"]

        @blk.sync
        def _(sy):
            b.wd(sy, "dAO0"); b.wd(sy, "dAO1")
            for t in range(NT):
                b.w(sy, "dve", a9_y10 + (t - 2) + 1 if t >= 2 else 0)
                b.dma(f"dY1L{t % 2}", sy, mf0 if t % 2 == 0 else mf1,
                      x_h[t * P:(t + 1) * P, :])
                b.dma(f"dY1L{t % 2}", sy, flipb[t % 2],
                      scr_ao[:, t * P:(t + 1) * P], transpose=True)
                LD[f"y1l{t}"] = (f"dY1L{t % 2}",
                                 b.dma(f"dY1L{t % 2}", sy, flipb[t % 2],
                                       scr_ao[:, t * P:(t + 1) * P], transpose=True))

        a9_yact0 = b.c["act"]

        @blk.scalar
        def _(ac):
            for t in range(NT):
                b.wd(ac, *LD[f"y1l{t}"])
                mf = mf0 if t % 2 == 0 else mf1
                b.inc("act", ac.activation(mf, mf, AF.Identity,
                                           scale=c_rho1[:, t:t + 1],
                                           bias=c_nmr1[:, t:t + 1]))

        @blk.vector
        def _(ve):
            for t in range(NT):
                b.w(ve, "act", a9_yact0 + t + 1)
                ve.tensor_reduce(c_junk, flipb[t % 2], axis=AX.X, op=ALU.add)
                ve.memset(c_junk, 0.0)
                i = ve.tensor_add(resid[:, t, :], mf0 if t % 2 == 0 else mf1, flipb[t % 2])
                b.inc("dve", i)
            LD["y1"] = b.c["dve"]

        if "d_y1" in DBG:
            @blk.sync
            def _(sy):
                b.w(sy, "dve", LD["y1"])
                b.dma("dDBG", sy, DBG["d_y1"].ap().rearrange("(t p) d -> p t d", p=P),
                      resid[:, :, :])

        mark("a9")
        col_reset()
        c_msq = cols(16)
        c_rh = cols(16)

        @blk.sync
        def _(sy):
            b.w(sy, "pe", LD["a9_pe"])
            b.w(sy, "dve", LD["y1"])
            LD["mnwl"] = ("dMNW", b.dma("dMNW", sy, mf0, mnw_h[:, :]))

        @blk.scalar
        def _(ac):
            b.wd(ac, *LD["mnwl"])
            b.w(ac, "pe", LD["a9_pe"])
            ac.copy(junk_b1, mf0)
            i = ac.copy(mnw_bf, mf0)
            LD["mnwb"] = b.inc("act", i)

        @blk.vector
        def _(ve):
            b.w(ve, "dve", LD["y1"])
            b.w(ve, "pe", LD["a9_pe"])
            for t in range(NT):
                i = ve.scalar_tensor_tensor(junk_b1, resid[:, t, :], 1.0, resid[:, t, :],
                                            ALU.mult, ALU.mult, accum_out=c_msq[:, t:t + 1])
                b.inc("dve", i)
            ve.memset(c_junk, 0.0)
            ve.tensor_scalar(c_rh, c_msq, 1.0 / D, EPS, ALU.mult, ALU.add)
            LD["b1ms"] = b.inc("dve", verify(ve, c_rh))

        b1_hv0 = None

        @blk.scalar
        def _(ac):
            b.w(ac, "dve", LD["b1ms"])
            ac.activation(c_rh, c_rh, AF.Ln)
            i = ac.activation(c_rh, c_rh, AF.Exp, scale=-0.5)
            LD["b1rh"] = b.inc("act", i)

        b1_hv0 = b.c["dve"]
        b1_act1 = b.c["act"]

        @blk.scalar
        def _(ac):
            b.w(ac, "act", LD["b1rh"])
            b.w(ac, "act", LD["mnwb"])
            for t in range(NT):
                if t >= 2:
                    b.w(ac, "dve", b1_hv0 + (t - 2) + 1)
                i = ac.activation(mf0 if t % 2 == 0 else mf1, resid[:, t, :], AF.Identity,
                                  scale=c_rh[:, t:t + 1])
                b.inc("act", i)
                LD[f"b1a{t}"] = b.c["act"]
                if t >= 1:
                    b.w(ac, "dve", b1_hv0 + (t - 1) + 1)
                    LD[f"b1w{t - 1}"] = (f"dH2{(t - 1) % 2}",
                                         b.dma(f"dH2{(t - 1) % 2}", ac, scr_h2[(t - 1) * P:t * P, :],
                                               mb0 if (t - 1) % 2 == 0 else mb1))
            b.w(ac, "dve", b1_hv0 + NT)
            b.dma("dH21", ac, scr_h2[(NT - 1) * P:NT * P, :], mb1)
            LD[f"b1w{NT - 1}"] = ("dH21", b.dma("dH21", ac, scr_h2[(NT - 1) * P:NT * P, :], mb1))

        @blk.vector
        def _(ve):
            for t in range(NT):
                b.w(ve, "act", LD[f"b1a{t}"])
                if t >= 2:
                    b.wd(ve, *LD[f"b1w{t - 2}"])
                i = ve.tensor_mul(mb0 if t % 2 == 0 else mb1, mf0 if t % 2 == 0 else mf1, mnw_bf)
                b.inc("dve", i)

        @blk.sync
        def _(sy):
            b.wd(sy, "dH20"); b.wd(sy, "dH21")
            for rep in range(2):
                for fc in range(8):
                    c = b.dma("dH2F", sy, h2_fm[:, fc, :],
                              scr_h2[:, fc * P:(fc + 1) * P], transpose=True)
            LD["h2fm"] = ("dH2F", c)

        if "d_h2" in DBG:
            @blk.sync
            def _(sy):
                b.wd(sy, "dH20"); b.wd(sy, "dH21")
                b.dma("dDBG", sy, DBG["d_h2"][:, :], scr_h2[:, :])

        b2_pe0 = b.c["pe"]
        b2_act0 = b.c["act"]
        b2_dve0 = b.c["dve"]

        def b2_pe_mg(g, sub, ocl):
            return b2_pe0 + (g * 4 + sub) * 16 + 2 * ocl + 1

        def b2_pe_mu(g, sub, ocl):
            return b2_pe0 + (g * 4 + sub) * 16 + 2 * ocl + 2

        def b2_pe_md(g, sub, outc):
            return b2_pe0 + (g * 4 + sub) * 16 + 8 + outc + 1

        def b2_act_s(j):
            return b2_act0 + j + 1

        def b2_dve_g(j):
            g = j // 16
            sub = (j % 16) // 4
            return b2_dve0 + g * 64 + sub * 12 + (j % 4) + 1

        def b2_dve_d(jj):
            g = jj // 32
            sub = (jj % 32) // 8
            return b2_dve0 + g * 64 + sub * 12 + 4 + (jj % 8) + 1

        def wp2_dve(i5):
            ch = i5 // 8
            return b2_dve0 + ch * 64 + 48 + 2 * (i5 % 8) + 2

        @blk.gpsimd
        def _(gp):
            for g in range(8):
                if g == 0:
                    b.w(gp, "pe", LD["a6_pe"])
                    b.w(gp, "pe", LD["a9_pe"])
                if g >= 2:
                    b.w(gp, "pe", b2_pe0 + (g - 1) * 64)
                b.dma(f"dB2W{g % 2}", gp, wg_view(g % 2),
                      wg_h[:, g * 512:(g + 1) * 512].rearrange("(a p) o -> p a o", p=P))
                b.dma(f"dB2W{g % 2}", gp, wu_view(g % 2),
                      wu_h[:, g * 512:(g + 1) * 512].rearrange("(a p) o -> p a o", p=P))
                LD[f"b2wl{g}"] = (f"dB2W{g % 2}",
                                  b.dma(f"dB2W{g % 2}", gp, wd_view[g % 2],
                                        wd_h[g * 512:(g + 1) * 512, :].rearrange("(a p) o -> p a o", p=P)))

        @blk.tensor
        def _(te):
            b.wd(te, *LD["h2fm"])
            for g in range(8):
                b.wd(te, *LD[f"b2wl{g}"])
                wgv, wuv, wdv = wg_view(g % 2), wu_view(g % 2), wd_view[g % 2]
                if g == 0:
                    if bank_last[0]:
                        b.w(te, bank_last[0][0], bank_last[0][1])
                    te.matmul(ps[0][:, :], wdv[:, 3, 0:P], h2_fm[:, 7, 0:512],
                              start=True, stop=True)
                    te.matmul(ps[0][:, :], h2_fm[:, 7, 0:P], wgv[:, 7, 0:512],
                              start=True, stop=True)
                for sub in range(4):
                    for ocl in range(4):
                        gb, ub = ps[ocl % 2], ps[2 + ocl % 2]
                        j = (g * 4 + sub) * 4 + ocl
                        if j >= 2:
                            b.w(te, "act", b2_act_s(j - 2))
                        elif bank_last[ocl % 2]:
                            b.w(te, bank_last[ocl % 2][0], bank_last[ocl % 2][1])
                        for kc in range(8):
                            i = te.matmul(gb[:, :], wgv[:, kc, ocl * P:(ocl + 1) * P],
                                          h2_fm[:, kc, sub * 512:(sub + 1) * 512],
                                          start=(kc == 0), stop=(kc == 7))
                        b.inc("pe", i)
                        if j >= 2:
                            b.w(te, "dve", b2_dve_g(j - 2))
                        elif bank_last[2 + ocl % 2]:
                            b.w(te, bank_last[2 + ocl % 2][0], bank_last[2 + ocl % 2][1])
                        for kc in range(8):
                            i = te.matmul(ub[:, :], wuv[:, kc, ocl * P:(ocl + 1) * P],
                                          h2_fm[:, kc, sub * 512:(sub + 1) * 512],
                                          start=(kc == 0), stop=(kc == 7))
                        b.inc("pe", i)
                    for outc in range(8):
                        db = ps[4 + outc % 2]
                        jj = (g * 4 + sub) * 8 + outc
                        if jj >= 2:
                            b.w(te, "dve", b2_dve_d(jj - 2))
                        elif bank_last[4 + outc % 2]:
                            b.w(te, bank_last[4 + outc % 2][0], bank_last[4 + outc % 2][1])
                        for ocl in range(4):
                            b.w(te, "dve", b2_dve_g((g * 4 + sub) * 4 + ocl))
                            i = te.matmul(db[:, :], wdv[:, ocl, outc * P:(outc + 1) * P],
                                          g_tile[:, ocl, :], start=(ocl == 0), stop=(ocl == 3))
                        b.inc("pe", i)
        LD["b2_pe"] = b.c["pe"]

        @blk.scalar
        def _(ac):
            for g in range(8):
                for sub in range(4):
                    for ocl in range(4):
                        j = (g * 4 + sub) * 4 + ocl
                        b.w(ac, "pe", b2_pe_mg(g, sub, ocl))
                        if j >= 2:
                            b.w(ac, "dve", b2_dve_g(j - 2))
                        i = ac.activation(sg_bufs[ocl % 2], ps[ocl % 2][:, :], AF.Silu)
                        b.inc("act", i)

        @blk.sync
        def _(sy):
            for i5 in range(64):
                if i5 < 2:
                    b.w(sy, "dve", b1_hv0 + NT)
                    b.wd(sy, "dH20"); b.wd(sy, "dH21")
                else:
                    b.w(sy, "dve", wp2_dve(i5 - 2))
                src = w1_h if i5 < 32 else w2_h
                LD[f"wp2l{i5}"] = (f"dWP2{i5 % 2}",
                                   b.dma(f"dWP2{i5 % 2}", sy, (mf0 if i5 % 2 == 0 else mf1),
                                         wblk(src, i5)))
                if i5 >= 1:
                    b.w(sy, "dve", wp2_dve(i5 - 1))
                    dst_s = scr_w1q if (i5 - 1) < 32 else scr_w2q
                    LD[f"wp2w{i5 - 1}"] = ("dWP2W", b.dma("dWP2W", sy,
                                                          wblk(dst_s, i5 - 1), mb1[:, :]))
            b.w(sy, "dve", wp2_dve(63))
            LD["wp2w63"] = ("dWP2W", b.dma("dWP2W", sy, wblk(scr_w2q, 63), mb1[:, :]))

        for g in range(8):
            def _mk(gg):
                @blk.vector
                def _(ve):
                    for sub in range(4):
                        for ocl in range(4):
                            j = (gg * 4 + sub) * 4 + ocl
                            b.w(ve, "act", b2_act_s(j))
                            b.w(ve, "pe", b2_pe_mu(gg, sub, ocl))
                            i = ve.tensor_mul(g_tile[:, ocl, :], sg_bufs[ocl % 2],
                                              ps[2 + ocl % 2][:, :])
                            b.inc("dve", i)
                        for outc in range(8):
                            jj = (gg * 4 + sub) * 8 + outc
                            b.w(ve, "pe", b2_pe_md(gg, sub, outc))
                            db = ps[4 + outc % 2]
                            if gg == 0:
                                i = ve.tensor_copy(y2acc[:, outc, sub * 512:(sub + 1) * 512],
                                                   db[:, :])
                            else:
                                i = ve.tensor_add(y2acc[:, outc, sub * 512:(sub + 1) * 512],
                                                  y2acc[:, outc, sub * 512:(sub + 1) * 512],
                                                  db[:, :])
                            b.inc("dve", i)

                @blk.vector
                def _(ve):
                    for k5 in range(8):
                        i5 = gg * 8 + k5
                        wi = 0 if i5 < 32 else 1
                        wwb = mf0 if i5 % 2 == 0 else mf1
                        b.wd(ve, *LD[f"wp2l{i5}"])
                        if i5 >= 1:
                            b.wd(ve, *LD[f"wp2w{i5 - 1}"])
                        b.inc("dve", ve.tensor_single_scalar(mb0[:, :], wwb,
                                                             c_nth[:, wi:wi + 1], ALU.is_le))
                        b.inc("dve", ve.scalar_tensor_tensor(mb1[:, :], wwb,
                                                             c_th[:, wi:wi + 1], mb0[:, :],
                                                             ALU.is_ge, ALU.subtract))
            _mk(g)
        LD["b2"] = b.c["dve"]
        assert (not blk.on) or LD["b2"] == b2_dve0 + 512, (LD["b2"], b2_dve0)
        for i in range(4):
            bank_last[i] = ("dve", LD["b2"])
        bank_last[4] = ("dve", LD["b2"])
        bank_last[5] = ("dve", LD["b2"])

        @blk.scalar
        def _(ac):
            b.w(ac, "dve", LD["b2"])
            b.dma("dSCRMO", ac, scr_mo.ap().rearrange("(a p) t -> p a t", p=P), y2acc[:, :, :])
            LD["scrmo"] = ("dSCRMO", b.dma("dSCRMO", ac,
                                           scr_mo.ap().rearrange("(a p) t -> p a t", p=P),
                                           y2acc[:, :, :]))

        if "d_mo" in DBG:
            @blk.sync
            def _(sy):
                b.wd(sy, *LD["scrmo"])
                b.dma("dDBG", sy, DBG["d_mo"][:, :], scr_mo[:, :])

        b3_dve0 = b.c["dve"]

        @blk.sync
        def _(sy):
            b.wd(sy, *LD["scrmo"])
            for t in range(NT):
                b.w(sy, "dve", b3_dve0 + (t - 2) + 1 if t >= 2 else 0)
                b.dma(f"dB3L{t % 2}", sy, flipb[t % 2],
                      scr_mo[:, t * P:(t + 1) * P], transpose=True)
                LD[f"b3l{t}"] = (f"dB3L{t % 2}",
                                 b.dma(f"dB3L{t % 2}", sy, flipb[t % 2],
                                       scr_mo[:, t * P:(t + 1) * P], transpose=True))

        @blk.vector
        def _(ve):
            for t in range(NT):
                b.wd(ve, *LD[f"b3l{t}"])
                ve.tensor_reduce(c_junk, flipb[t % 2], axis=AX.X, op=ALU.add)
                ve.memset(c_junk, 0.0)
                i = ve.tensor_add(resid[:, t, :], resid[:, t, :], flipb[t % 2])
                b.inc("dve", i)
            LD["y2"] = b.c["dve"]

        mark("b3")
        col_reset()
        c_sum4 = cols(16); c_ssq4 = cols(16); c_m4 = cols(16); c_v4 = cols(16)
        c_rh4 = cols(16); c_nm4 = cols(16)

        @blk.vector
        def _(ve):
            b.w(ve, "dve", LD["y2"])
            b.w(ve, "pe", LD["b2_pe"])
            for t in range(NT):
                ve.reduce_sum(c_sum4[:, t:t + 1], resid[:, t, :], axis=AX.X)
                i = ve.scalar_tensor_tensor(S3[:, 5, 0:1024], resid[:, t, :], 1.0,
                                            resid[:, t, :],
                                            ALU.mult, ALU.mult, accum_out=c_ssq4[:, t:t + 1])
                b.inc("dve", i)
            ve.tensor_scalar(c_m4, c_sum4, 1.0 / D, None, ALU.mult)
            ve.tensor_scalar(c_v4, c_ssq4, 1.0 / D, None, ALU.mult)
            ve.tensor_mul(c_rh4, c_m4, c_m4)
            ve.memset(c_junk, 0.0)
            ve.tensor_sub(c_v4, c_v4, c_rh4)
            ve.memset(c_junk, 0.0)
            ve.tensor_scalar(c_v4, c_v4, EPS, None, ALU.add)
            LD["b4v"] = b.inc("dve", verify(ve, c_v4))

        @blk.scalar
        def _(ac):
            b.w(ac, "dve", LD["b4v"])
            ac.activation(c_rh4, c_v4, AF.Ln)
            i = ac.activation(c_rh4, c_rh4, AF.Exp, scale=-0.5)
            LD["b4rh"] = b.inc("act", i)

        @blk.vector
        def _(ve):
            b.w(ve, "act", LD["b4rh"])
            ve.scalar_tensor_tensor(c_nm4, c_m4, -1.0, c_rh4, ALU.mult, ALU.mult)
            LD["b4nm"] = b.inc("dve", verify(ve, c_nm4))

        b4_x0 = b.c["dve"]

        @blk.sync
        def _(sy):
            for t in range(NT):
                b.w(sy, "dve", b4_x0 + (t - 2) + 1 if t >= 2 else 0)
                LD[f"b4l{t}"] = (f"dB4L{t % 2}",
                                 b.dma(f"dB4L{t % 2}", sy, mf0 if t % 2 == 0 else mf1,
                                       x_h[t * P:(t + 1) * P, :]))

        @blk.scalar
        def _(ac):
            b.w(ac, "dve", LD["b4nm"])
            for t in range(NT):
                if t >= 1:
                    b.w(ac, "dve", b4_x0 + (t - 1) + 1)
                i = ac.activation(resid[:, t, :], resid[:, t, :], AF.Identity,
                                  scale=c_rh4[:, t:t + 1], bias=c_nm4[:, t:t + 1])
                b.inc("act", i)
                LD[f"b4a{t}"] = b.c["act"]

        @blk.vector
        def _(ve):
            for t in range(NT):
                b.w(ve, "act", LD[f"b4a{t}"])
                b.wd(ve, *LD[f"b4l{t}"])
                ve.tensor_reduce(c_junk, mf0 if t % 2 == 0 else mf1, axis=AX.X, op=ALU.add)
                ve.memset(c_junk, 0.0)
                i = ve.tensor_add(resid[:, t, :], resid[:, t, :], mf0 if t % 2 == 0 else mf1)
                b.inc("dve", i)
            LD["x1"] = b.c["dve"]

        if "d_x1" in DBG:
            @blk.sync
            def _(sy):
                b.w(sy, "dve", LD["x1"])
                b.dma("dDBG", sy, DBG["d_x1"].ap().rearrange("(t p) d -> p t d", p=P),
                      resid[:, :, :])

        col_reset()
        c_su = cols(16); c_sq = cols(16); c_m1 = cols(16); c_v1 = cols(16)
        c_r1 = cols(16); c_n1 = cols(16); c_mx = cols(16); c_rc = cols(16)
        c_iz = cols(16)

        @blk.vector
        def _(ve):
            b.w(ve, "dve", LD["x1"])
            for t in range(NT):
                ve.reduce_sum(c_su[:, t:t + 1], resid[:, t, :], axis=AX.X)
                i = ve.scalar_tensor_tensor(S3[:, 5, 0:1024], resid[:, t, :], 1.0,
                                            resid[:, t, :],
                                            ALU.mult, ALU.mult, accum_out=c_sq[:, t:t + 1])
                b.inc("dve", i)
            ve.tensor_scalar(c_m1, c_su, 1.0 / D, None, ALU.mult)
            ve.tensor_scalar(c_v1, c_sq, 1.0 / D, None, ALU.mult)
            ve.tensor_mul(c_r1, c_m1, c_m1)
            ve.memset(c_junk, 0.0)
            ve.tensor_sub(c_v1, c_v1, c_r1)
            ve.memset(c_junk, 0.0)
            ve.tensor_scalar(c_v1, c_v1, EPS, None, ALU.add)
            LD["c1v"] = b.inc("dve", verify(ve, c_v1))

        @blk.scalar
        def _(ac):
            b.w(ac, "dve", LD["c1v"])
            ac.activation(c_r1, c_v1, AF.Ln)
            i = ac.activation(c_r1, c_r1, AF.Exp, scale=-0.5)
            LD["c1rh"] = b.inc("act", i)

        @blk.vector
        def _(ve):
            b.w(ve, "act", LD["c1rh"])
            ve.scalar_tensor_tensor(c_n1, c_m1, -1.0, c_r1, ALU.mult, ALU.mult)
            LD["c1nm"] = b.inc("dve", verify(ve, c_n1))

        c1_mx0 = b.c["dve"]

        @blk.scalar
        def _(ac):
            b.w(ac, "dve", LD["c1nm"])
            for t in range(NT):
                b.w(ac, "dve", c1_mx0 + (t - 2) + 1 if t >= 2 else 0)
                i = ac.activation(mf0 if t % 2 == 0 else mf1, resid[:, t, :], AF.Identity,
                                  scale=c_r1[:, t:t + 1], bias=c_n1[:, t:t + 1])
                b.inc("act", i)
                LD[f"c1z{t}"] = b.c["act"]

        @blk.vector
        def _(ve):
            for t in range(NT):
                b.w(ve, "act", LD[f"c1z{t}"])
                i = ve.reduce_max(c_mx[:, t:t + 1], mf0 if t % 2 == 0 else mf1,
                                  axis=AX.X, apply_absolute_value=True)
                b.inc("dve", i)
            ve.memset(c_junk, 0.0)
            i = ve.tensor_scalar(c_mx, c_mx, EPS, None, ALU.max)
            b.inc("dve", i)
            b.w(ve, "dve", b.c["dve"])
            ve.reciprocal(c_rc, c_mx)
            ve.tensor_scalar(c_iz, c_mx, 1.0 / 127.0, None, ALU.mult)
            LD["c1sc"] = b.inc("dve", verify(ve, c_rc, c_iz))

        c1_q0 = b.c["dve"]

        zst16 = [S3[:, q // 2, (q % 2) * 1024:(q % 2) * 1024 + 1024] for q in range(16)]

        @blk.scalar
        def _(ac):
            b.w(ac, "dve", LD["c1sc"])
            for t in range(NT):
                if t >= 1:
                    tp = t - 1
                    b.w(ac, "dve", c1_q0 + tp + 1)
                    b.inc("act", ac.copy(mf1.bitcast(I32), mf1))
                    b.inc("act", ac.copy(zst16[tp], mf1.bitcast(I32)))
                    LD[f"c1c{tp}"] = b.c["act"]
                i = ac.activation(mf0, resid[:, t, :], AF.Identity,
                                  scale=c_r1[:, t:t + 1], bias=c_n1[:, t:t + 1])
                b.inc("act", i)
                LD[f"c1za{t}"] = b.c["act"]
            tp = NT - 1
            b.w(ac, "dve", c1_q0 + tp + 1)
            b.inc("act", ac.copy(mf1.bitcast(I32), mf1))
            b.inc("act", ac.copy(zst16[tp], mf1.bitcast(I32)))
            LD[f"c1c{tp}"] = b.c["act"]
            i = ac.copy(mb0, mf0)
            LD["c1tail"] = b.inc("act", i)

        @blk.vector
        def _(ve):
            for t in range(NT):
                b.w(ve, "act", LD[f"c1za{t}"])
                if t >= 1:
                    b.w(ve, "act", LD[f"c1c{t - 1}"])
                i = ve.tensor_scalar(mf1, mf0, c_rc[:, t:t + 1], 127.0, ALU.mult, ALU.mult)
                b.inc("dve", i)
                LD[f"c1s{t}"] = b.c["dve"]

        mark("c1")
        @blk.tensor
        def _(te):
            b.w(te, "dve", LD["c1sc"])
            b.w(te, "dve", LD["b2"])
            b.w(te, bank_last[5][0], bank_last[5][1])
            i = te.transpose(ps[5][0:16, 0:P], c_iz, idf_sb[:, :])
            b.inc("pe", i)
            LD["m1t"] = b.c["pe"]

        @blk.vector
        def _(ve):
            b.w(ve, "pe", LD["m1t"])
            ve.tensor_copy(trow[0:16, 0:P], ps[5][0:16, 0:P])
            LD["m1r"] = b.inc("dve", verify(ve, trow[0:16, 0:P]))

        @blk.sync
        def _(sy):
            b.w(sy, "dve", LD["m1r"])
            LD["m1row"] = ("dM1R", b.dma("dM1R", sy,
                                         mrow_f[0:1, 0:N].rearrange("p (a d) -> p a d", d=P),
                                         trow[0:16, 0:P]))

        @blk.vector
        def _(ve):
            b.wd(ve, *LD["m1row"])
            ve.tensor_reduce(c_junk[0:1, :], mrow_f[0:1, 0:N], axis=AX.X, op=ALU.add)
            ve.memset(c_junk, 0.0)
            i = ve.tensor_scalar(mrow_f[0:1, 0:N], mrow_f[0:1, 0:N], c_s[0:1, 0:1], None, ALU.mult)
            b.inc("dve", i)
            b.w(ve, "dve", b.c["dve"])
            ve.tensor_copy(mrow_b[0:1, 0:N], mrow_f[0:1, 0:N])
            LD["m1rb"] = b.inc("dve", verify(ve, mrow_b[0:1, 0:N]))

        m1_e0 = b.c["act"]

        @blk.tensor
        def _(te):
            b.w(te, "dve", LD["m1rb"])
            b.w(te, bank_last[6][0], bank_last[6][1])
            for nb in range(4):
                if nb >= 1:
                    b.w(te, "act", m1_e0 + nb)
                i = te.matmul(ps[6][:, :], onesr_sb[0:1, :], mrow_b[0:1, nb * 512:(nb + 1) * 512],
                              start=True, stop=True)
                b.inc("pe", i)
                LD[f"m1b{nb}"] = b.c["pe"]

        @blk.scalar
        def _(ac):
            for nb in range(4):
                b.w(ac, "pe", LD[f"m1b{nb}"])
                b.inc("act", ac.copy(m1_sb[:, nb * 512:(nb + 1) * 512], ps[6][:, :]))
            LD["m1"] = b.c["act"]

        if "d_z" in DBG:
            @blk.sync
            def _(sy):
                b.w(sy, "act", LD["c1tail"])
                b.dma("dDBG", sy, DBG["d_z"].ap().rearrange("(a p) d -> p a d", p=P),
                      S3[:, :, :].rearrange("p a (q d) -> p (a q) d", d=1024))
        if "d_w1q" in DBG:
            @blk.sync
            def _(sy):
                b.wd(sy, "dWP2W")
                b.dma("dDBG", sy, DBG["d_w1q"][:, :], scr_w1q[:, :])
                b.dma("dDBG", sy, DBG["d_w2q"][:, :], scr_w2q[:, :])

        @blk.sync
        def _(sy):
            b.w(sy, "pe", LD["b2_pe"])
            for t in range(NT):
                b.w(sy, "act", LD[f"c1za{t + 1}"] if t + 1 < NT else LD["c1tail"])
                for fc in range(8):
                    c = b.dma("dZF", sy, zint_fm[:, fc, t * P:(t + 1) * P],
                              zst16[t][:, fc * P:(fc + 1) * P], transpose=True)
            LD["zfm"] = ("dZF", c)

        @blk.vector
        def _(ve):
            b.w(ve, "dve", LD["b2"])
            b.inc("dve", ve.memset(gmax, 0.0))
            i = ve.memset(gmin, 0.0)
            LD["gmax0"] = b.inc("dve", i)

        _c2pe0 = b.c["pe"]

        @blk.sync
        def _(sy):
            b.w(sy, "pe", LD["b2_pe"])
            b.wd(sy, "dWP2W")
            for ho in range(2):
                if ho == 1:
                    b.w(sy, "pe", _c2pe0 + 64)
                for ic in range(8):
                    c = b.dma(f"dW1T{ho}", sy, w1T[:, ic, :],
                              scr_w1q[ho * N:(ho + 1) * N, ic * P:(ic + 1) * P], transpose=True)
                LD[f"w1T{ho}"] = (f"dW1T{ho}", c)

        @blk.vector
        def _(ve):
            b.wd(ve, *LD["zfm"])
            b.wd(ve, *LD["w1T0"])
            ve.tensor_reduce(c_junk, zint_fm[:, 7, :], axis=AX.X, op=ALU.add)
            ve.memset(c_junk, 0.0)
            i = ve.tensor_reduce(c_junk, w1T[:, 7, :], axis=AX.X, op=ALU.add)
            LD["c2set"] = b.inc("dve", i)

        c2_pe0 = b.c["pe"]
        assert (not blk.on) or c2_pe0 == _c2pe0
        c2_dve0 = b.c["dve"]
        c2_act0 = b.c["act"]

        @blk.tensor
        def _(te):
            b.w(te, "dve", LD["c2set"])
            b.w(te, "act", LD["m1"])
            b.w(te, "dve", LD["gmax0"])
            for ho in range(2):
                b.wd(te, *LD[f"w1T{ho}"])
                if ho == 1:
                    b.w(te, "dve", c2_dve0 + 15 * 6 + 1)
                    te.matmul(ps[0][:, :], w1T[:, 7, 0:P], zint_fm[:, 7, 0:512],
                              start=True, stop=True)
                for oc in range(16):
                    occ = ho * 16 + oc
                    for nb in range(4):
                        j = occ * 4 + nb
                        r = j % 4
                        if j >= 4:
                            b.w(te, "dve", c2_dve0 + ((j - 4) // 4) * 6 + ((j - 4) % 4) + 1)
                        elif bank_last[r]:
                            b.w(te, bank_last[r][0], bank_last[r][1])
                        for ic in range(8):
                            i = te.matmul(ps[r][:, :], w1T[:, ic, oc * P:(oc + 1) * P],
                                          zint_fm[:, ic, nb * 512:(nb + 1) * 512],
                                          start=(ic == 0), stop=(ic == 7))
                        b.inc("pe", i)
        LD["c2_pe"] = b.c["pe"]

        @blk.vector
        def _(ve):
            for occ in range(32):
                oce = occ % 2
                for nb in range(4):
                    j = occ * 4 + nb
                    b.w(ve, "pe", c2_pe0 + j + 1)
                    if occ >= 2 and nb == 0:
                        b.wd(ve, f"dGW{occ % 2}", 16 * ((occ - 2) // 2 + 1))
                    i = ve.tensor_tensor(g2row[oce][:, nb * 512:(nb + 1) * 512],
                                         ps[j % 4][:, :], m1_sb[:, nb * 512:(nb + 1) * 512],
                                         ALU.mult)
                    b.inc("dve", i)
                b.w(ve, "act", c2_act0 + occ * 4 + 4)
                b.inc("dve", ve.tensor_max(gmax, gmax, g2row[oce][:, :]))
                i = ve.tensor_tensor(gmin, gmin, g2row[oce][:, :], ALU.min)
                b.inc("dve", i)
        LD["c2gm"] = b.c["dve"]

        @blk.scalar
        def _(ac):
            for occ in range(32):
                oce = occ % 2
                for nb in range(4):
                    j = occ * 4 + nb
                    b.w(ac, "dve", c2_dve0 + occ * 6 + nb + 1)
                    i = ac.activation(g2row[oce][:, nb * 512:(nb + 1) * 512],
                                      g2row[oce][:, nb * 512:(nb + 1) * 512], AF.Gelu)
                    b.inc("act", i)
                b.w(ac, "dve", c2_dve0 + occ * 6 + 6)
                ac.copy(mf0, g2row[oce][:, 0:1024])
                LD[f"c2w{occ}"] = (f"dGW{occ % 2}",
                                   b.dma(f"dGW{occ % 2}", ac, scr_g[occ * P:(occ + 1) * P, :],
                                         g2row[oce][:, :]))
        for i in range(4):
            bank_last[i] = ("dve", LD["c2gm"])

        if "d_g" in DBG:
            @blk.sync
            def _(sy):
                b.wd(sy, "dGW0"); b.wd(sy, "dGW1")
                b.dma("dDBG", sy, DBG["d_g"][:, :], scr_g[:, :])

        mark("c2")
        col_reset()
        c_gmx = cols(16)
        c_gsc = cols(16)

        @blk.vector
        def _(ve):
            b.w(ve, "dve", LD["c2gm"])
            b.inc("dve", ve.tensor_scalar(gmin, gmin, -1.0, None, ALU.mult))
            ve.tensor_max(gmax, gmax, gmin)
            LD["c2gm2"] = b.inc("dve", verify(ve, gmax))
        c3_rd0 = b.c["dve"]

        @blk.tensor
        def _(te):
            b.w(te, "dve", LD["c2gm2"])
            b.w(te, bank_last[4][0], bank_last[4][1])
            b.w(te, bank_last[5][0], bank_last[5][1])
            for t in range(NT):
                r5 = ps[4 + t % 2]
                b.w(te, "dve", c3_rd0 + (t - 2) + 1 if t >= 2 else 0)
                i = te.transpose(r5[0:P, 0:P], gmax[:, t * P:(t + 1) * P], idf_sb[:, :])
                b.inc("pe", i)
                LD[f"c3t{t}"] = b.c["pe"]

        @blk.vector
        def _(ve):
            for t in range(NT):
                b.w(ve, "pe", LD[f"c3t{t}"])
                i = ve.reduce_max(c_gmx[:, t:t + 1], ps[4 + t % 2][:, 0:P], axis=AX.X)
                b.inc("dve", i)
            ve.memset(c_junk, 0.0)
            i = ve.tensor_scalar(c_gmx, c_gmx, EPS, None, ALU.max)
            b.inc("dve", i)
            b.w(ve, "dve", b.c["dve"])
            ve.reciprocal(c_gsc, c_gmx)
            LD["c3sc"] = b.inc("dve", verify(ve, c_gsc))

        @blk.tensor
        def _(te):
            b.w(te, "dve", LD["c3sc"])
            i = te.transpose(ps[5][0:16, 0:P], c_gsc, idf_sb[:, :])
            b.inc("pe", i)
            LD["m2t"] = b.c["pe"]

        @blk.vector
        def _(ve):
            b.w(ve, "pe", LD["m2t"])
            ve.tensor_copy(trow[0:16, 0:P], ps[5][0:16, 0:P])
            LD["m2r"] = b.inc("dve", verify(ve, trow[0:16, 0:P]))

        @blk.sync
        def _(sy):
            b.w(sy, "dve", LD["m2r"])
            LD["m2row"] = ("dM1R", b.dma("dM1R", sy,
                                         mrow_f[0:1, 0:N].rearrange("p (a d) -> p a d", d=P),
                                         trow[0:16, 0:P]))

        @blk.vector
        def _(ve):
            b.wd(ve, *LD["m2row"])
            ve.tensor_reduce(c_junk[0:1, :], mrow_f[0:1, 0:N], axis=AX.X, op=ALU.add)
            ve.memset(c_junk, 0.0)
            i = ve.tensor_scalar(mrow_f[0:1, 0:N], mrow_f[0:1, 0:N], 127.0, None, ALU.mult)
            b.inc("dve", i)
            b.w(ve, "dve", b.c["dve"])
            ve.tensor_copy(mrow_b[0:1, 0:N], mrow_f[0:1, 0:N])
            LD["m2rb"] = b.inc("dve", verify(ve, mrow_b[0:1, 0:N]))

        m2_e0 = b.c["act"]

        @blk.tensor
        def _(te):
            b.w(te, "dve", LD["m2rb"])
            b.w(te, bank_last[6][0], bank_last[6][1])
            for nb in range(4):
                if nb >= 1:
                    b.w(te, "act", m2_e0 + nb)
                i = te.matmul(ps[6][:, :], onesr_sb[0:1, :], mrow_b[0:1, nb * 512:(nb + 1) * 512],
                              start=True, stop=True)
                b.inc("pe", i)
                LD[f"m2b{nb}"] = b.c["pe"]

        @blk.scalar
        def _(ac):
            for nb in range(4):
                b.w(ac, "pe", LD[f"m2b{nb}"])
                b.inc("act", ac.copy(m2_sb[:, nb * 512:(nb + 1) * 512], ps[6][:, :]))
            LD["m2"] = b.c["act"]

        c4_dve0 = b.c["dve"]
        c4_act0 = b.c["act"]
        c4_pe0 = b.c["pe"]

        def c4_act_c(j):
            return c4_act0 + (j // 32) * 72 + 2 * (j % 32) + 2

        @blk.sync
        def _(sy):
            b.w(sy, "act", LD["m2"])
            b.wd(sy, "dGW0")
            b.wd(sy, "dGW1")
            for nb in range(4):
                for ic in range(32):
                    j = nb * 32 + ic
                    if j >= 2:
                        b.w(sy, "dve", c4_dve0 + (j - 2) + 1)
                        b.w(sy, "pe", c4_pe0 + (j - 2) + 1)
                    b.dma(f"dC4L{ic % 2}", sy, g2rb[ic % 2],
                          scr_g[ic * P:(ic + 1) * P, nb * 512:(nb + 1) * 512])
                    LD[f"c4l{j}"] = (f"dC4L{ic % 2}",
                                     b.dma(f"dC4L{ic % 2}", sy, w2T_buf[ic % 2],
                                           scr_w2q[:, ic * P:(ic + 1) * P], transpose=True))

        @blk.vector
        def _(ve):
            for nb in range(4):
                for ic in range(32):
                    j = nb * 32 + ic
                    b.wd(ve, *LD[f"c4l{j}"])
                    b.w(ve, "act", c4_act_c(j - 1) if j >= 1 else 0)
                    i = ve.tensor_tensor(mf1[:, 0:512], g2rb[ic % 2],
                                         m2_sb[:, nb * 512:(nb + 1) * 512], ALU.mult)
                    b.inc("dve", i)
                    LD[f"c4z{j}"] = b.c["dve"]

        @blk.scalar
        def _(ac):
            for nb in range(4):
                for ic in range(32):
                    j = nb * 32 + ic
                    b.w(ac, "dve", LD[f"c4z{j}"])
                    if j >= 32:
                        b.w(ac, "pe", c4_pe0 + (j - 32) + 1)
                    b.inc("act", ac.copy(mf1.bitcast(I32)[:, 0:512], mf1[:, 0:512]))
                    b.inc("act", ac.copy(gq_sb[:, ic, :], mf1.bitcast(I32)[:, 0:512]))
                b.w(ac, "pe", c4_pe0 + 32 * (nb + 1))
                for outc in range(8):
                    b.inc("act", ac.copy(f_fm[:, outc, nb * 512:(nb + 1) * 512], ps[outc][:, :]))
            LD["c4_act"] = b.c["act"]

        @blk.tensor
        def _(te):
            for nb in range(4):
                for ic in range(32):
                    j = nb * 32 + ic
                    b.w(te, "act", c4_act_c(j))
                    if ic == 0:
                        if nb == 0:
                            for bb in range(8):
                                if bank_last[bb]:
                                    b.w(te, bank_last[bb][0], bank_last[bb][1])
                        else:
                            b.w(te, "act", c4_act0 + nb * 72)
                    for outc in range(8):
                        i = te.matmul(ps[outc][:, :], w2T_buf[ic % 2][:, outc * P:(outc + 1) * P],
                                      gq_sb[:, ic, :], start=(ic == 0), stop=(ic == 31))
                    b.inc("pe", i)
                LD[f"c4pe{nb}"] = b.c["pe"]

        mark("c4")
        @blk.scalar
        def _(ac):
            b.w(ac, "act", LD["c4_act"])
            b.dma("dSCRF", ac, scr_f.ap().rearrange("(a p) t -> p a t", p=P), f_fm[:, :, :])
            LD["scrf"] = ("dSCRF", b.dma("dSCRF", ac,
                                         scr_f.ap().rearrange("(a p) t -> p a t", p=P),
                                         f_fm[:, :, :]))

        if "d_f" in DBG:
            @blk.sync
            def _(sy):
                b.wd(sy, *LD["scrf"])
                b.dma("dDBG", sy, DBG["d_f"][:, :], scr_f[:, :])

        @blk.sync
        def _(sy):
            b.wd(sy, *LD["scrf"])
            for rep in range(2):
                for t in range(NT):
                    c = b.dma("dFF", sy, f_tm[:, t, :],
                              scr_f[:, t * P:(t + 1) * P], transpose=True)
            LD["ffm"] = ("dFF", c)

        col_reset()
        c_su5 = cols(16); c_sq5 = cols(16); c_m5 = cols(16); c_v5 = cols(16)
        c_r5 = cols(16); c_n5 = cols(16)

        @blk.vector
        def _(ve):
            b.wd(ve, *LD["ffm"])
            for t in range(NT):
                ve.reduce_sum(c_su5[:, t:t + 1], f_tm[:, t, :], axis=AX.X)
                i = ve.scalar_tensor_tensor(junk_c, f_tm[:, t, :], 1.0, f_tm[:, t, :],
                                            ALU.mult, ALU.mult, accum_out=c_sq5[:, t:t + 1])
                b.inc("dve", i)
            ve.tensor_scalar(c_m5, c_su5, 1.0 / D, None, ALU.mult)
            ve.tensor_scalar(c_v5, c_sq5, 1.0 / D, None, ALU.mult)
            ve.tensor_mul(c_r5, c_m5, c_m5)
            ve.memset(c_junk, 0.0)
            ve.tensor_sub(c_v5, c_v5, c_r5)
            ve.memset(c_junk, 0.0)
            ve.tensor_scalar(c_v5, c_v5, EPS, None, ALU.add)
            LD["c5v"] = b.inc("dve", verify(ve, c_v5))

        @blk.scalar
        def _(ac):
            b.w(ac, "dve", LD["c5v"])
            ac.activation(c_r5, c_v5, AF.Ln)
            i = ac.activation(c_r5, c_r5, AF.Exp, scale=-0.5)
            LD["c5rh"] = b.inc("act", i)

        @blk.vector
        def _(ve):
            b.w(ve, "act", LD["c5rh"])
            ve.scalar_tensor_tensor(c_n5, c_m5, -1.0, c_r5, ALU.mult, ALU.mult)
            LD["c5nm"] = b.inc("dve", verify(ve, c_n5))

        c5_x0 = b.c["dve"]

        @blk.scalar
        def _(ac):
            b.w(ac, "dve", LD["c5nm"])
            for t in range(NT):
                b.w(ac, "dve", c5_x0 + (t - 2) + 1 if t >= 2 else 0)
                i = ac.activation(mf0 if t % 2 == 0 else mf1, f_tm[:, t, :], AF.Identity,
                                  scale=c_r5[:, t:t + 1], bias=c_n5[:, t:t + 1])
                b.inc("act", i)
                LD[f"c5a{t}"] = b.c["act"]
                if t >= 2:
                    b.w(ac, "dve", c5_x0 + t)
                    b.dma("dOUT", ac, out_h[(t - 2) * P:(t - 1) * P, :], resid[:, t - 2, :])
            b.w(ac, "dve", c5_x0 + NT)
            b.dma("dOUT", ac, out_h[(NT - 2) * P:(NT - 1) * P, :], resid[:, NT - 2, :])
            b.w(ac, "dve", c5_x0 + NT + 1)
            b.dma("dOUT", ac, out_h[(NT - 1) * P:NT * P, :], resid[:, NT - 1, :])

        @blk.vector
        def _(ve):
            for t in range(NT):
                b.w(ve, "act", LD[f"c5a{t}"])
                i = ve.tensor_add(resid[:, t, :], resid[:, t, :], mf0 if t % 2 == 0 else mf1)
                b.inc("dve", i)
            LD["c5fin"] = b.inc("dve", verify(ve, resid[:, NT - 1, :]))

        @blk.sync
        def _(sy):
            b.wd(sy, "dOUT")
            if DBG:
                b.wd(sy, "dDBG")

    return nc


def _prep_inputs(inputs):
    x = np.asarray(inputs["L0_attn_hidden_states"], dtype=np.float32).reshape(16384, D)
    base = dict(
        wq=np.ascontiguousarray(np.asarray(inputs["L0_attn_Wq"], dtype=np.float32)),
        wk=np.ascontiguousarray(np.asarray(inputs["L0_attn_Wk"], dtype=np.float32)),
        wv=np.ascontiguousarray(np.asarray(inputs["L0_attn_Wv"], dtype=np.float32)),
        wo=np.ascontiguousarray(np.asarray(inputs["L0_attn_Wo"], dtype=np.float32)),
        wg=np.ascontiguousarray(np.asarray(inputs["L0_attn_Wgate"], dtype=np.float32)),
        wu=np.ascontiguousarray(np.asarray(inputs["L0_attn_Wup"], dtype=np.float32)),
        wd=np.ascontiguousarray(np.asarray(inputs["L0_attn_Wdown"], dtype=np.float32)),
        w1=np.ascontiguousarray(np.asarray(inputs["L0_ffn_w1"], dtype=np.float32)),
        w2=np.ascontiguousarray(np.asarray(inputs["L0_ffn_w2"], dtype=np.float32)),
        anw=np.ascontiguousarray(np.broadcast_to(
            np.asarray(inputs["L0_attn_attn_norm_w"], dtype=np.float32), (P, D))),
        mnw=np.ascontiguousarray(np.broadcast_to(
            np.asarray(inputs["L0_attn_mlp_norm_w"], dtype=np.float32), (P, D))),
        owc=np.ascontiguousarray(
            np.asarray(inputs["L0_attn_attn_out_norm_w"], dtype=np.float32).reshape(P, 1)),
        mask=np.ascontiguousarray(np.triu(np.ones((P, P), np.float32)).astype(ml_dtypes.bfloat16)),
        idf=np.ascontiguousarray(np.eye(P, dtype=np.float32)),
        onesr=np.ones((1, P), np.float32).astype(ml_dtypes.bfloat16),
        onesf=np.ones((P, P), np.float32),
    )
    in_maps = []
    for c in range(NCORES):
        m = dict(base)
        m["x"] = np.ascontiguousarray(x[c * N:(c + 1) * N, :])
        m["mcol"] = np.ascontiguousarray(
            np.broadcast_to((np.arange(NCORES) < c).astype(np.float32), (P, NCORES)))
        in_maps.append(m)
    return in_maps


def kernel(**inputs):
    in_maps = _prep_inputs(inputs)
    nc = build()
    res = run_bass_kernel_spmd(nc, in_maps, core_ids=list(range(NCORES)))
    out = np.concatenate([res.results[c]["out"] for c in range(NCORES)], axis=0)
    return out.reshape(1, 16384, D).astype(np.float32)


# version 13 (final)
# speedup vs baseline: 1.0955x; 1.0955x over previous
import numpy as np
import ml_dtypes
from contextlib import ExitStack

import concourse.bass as bass
import concourse.mybir as mybir
from concourse.bass_utils import run_bass_kernel_spmd

F32 = mybir.dt.float32
BF16 = mybir.dt.bfloat16
I32 = mybir.dt.int32
AF = mybir.ActivationFunctionType
ALU = mybir.AluOpType
AX = mybir.AxisListType

NCORES = 8
N = 2048
D = 1024
H = 8
NCH = 16
INTER = 4096
EPS = 1e-5
NT = 16
P = 128


class Ctx:

    def __init__(self, nc, stack):
        self.nc, self.stack = nc, stack
        self.h, self.c = {}, {}
        for s in ("pe", "act", "dve", "pool", "ccs"):
            self.h[s] = stack.enter_context(nc.semaphore(s))
            self.c[s] = 0
        self.dh, self.dc = {}, {}

    def inc(self, s, instr, n=1):
        self.c[s] += n
        instr.then_inc(self.h[s], n)
        return self.c[s]

    def w(self, eng, s, val):
        if val and val > 0:
            eng.wait_ge(self.h[s], val)

    def dsem(self, tag):
        if tag not in self.dh:
            self.dh[tag] = self.stack.enter_context(self.nc.semaphore(tag))
            self.dc[tag] = 0
        return self.dh[tag]

    def dma(self, tag, eng, out, in_, **kw):
        sem = self.dsem(tag)
        i = eng.dma_start(out=out, in_=in_, **kw)
        self.dc[tag] += 16
        i.then_inc(sem, 16)
        return self.dc[tag]

    def wd(self, eng, tag, val=None):
        sem = self.dsem(tag)
        v = self.dc[tag] if val is None else val
        if v and v > 0:
            eng.wait_ge(sem, v)


def build(dbg=(), stop=None, skip=()):
    nc = bass.Bass("TRN2", target_bir_lowering=False, detect_race_conditions=False)
    dt = nc.dram_tensor

    x_h = dt("x", [N, D], F32, kind="ExternalInput")
    wq_h = dt("wq", [D, D], F32, kind="ExternalInput")
    wk_h = dt("wk", [D, D], F32, kind="ExternalInput")
    wv_h = dt("wv", [D, D], F32, kind="ExternalInput")
    wo_h = dt("wo", [D, D], F32, kind="ExternalInput")
    wg_h = dt("wg", [D, INTER], F32, kind="ExternalInput")
    wu_h = dt("wu", [D, INTER], F32, kind="ExternalInput")
    wd_h = dt("wd", [INTER, D], F32, kind="ExternalInput")
    w1_h = dt("w1", [INTER, D], F32, kind="ExternalInput")
    w2_h = dt("w2", [D, INTER], F32, kind="ExternalInput")
    anw_h = dt("anw", [P, D], F32, kind="ExternalInput")
    mnw_h = dt("mnw", [P, D], F32, kind="ExternalInput")
    owc_h = dt("owc", [P, 1], F32, kind="ExternalInput")
    mask_h = dt("mask", [P, P], BF16, kind="ExternalInput")
    idf_h = dt("idf", [P, P], F32, kind="ExternalInput")
    onesr_h = dt("onesr", [1, P], BF16, kind="ExternalInput")
    onesf_h = dt("onesf", [P, P], F32, kind="ExternalInput")
    mcol_h = dt("mcol", [P, NCORES], F32, kind="ExternalInput")
    out_h = dt("out", [N, D], F32, kind="ExternalOutput")

    scr_warm = dt("scr_warm", [8, 16], F32, kind="Internal")
    scr_h1 = dt("scr_h1", [N, D], BF16, kind="Internal")
    scr_k = dt("scr_k", [D, N], BF16, kind="Internal")
    scr_sp = dt("scr_sp", [NCH * P, D], BF16, kind="Internal")
    scr_o = dt("scr_o", [N, D], BF16, kind="Internal")
    scr_ao = dt("scr_ao", [D, N], BF16, kind="Internal")
    scr_wo = dt("scr_wo", [D, D], BF16, kind="Internal")
    scr_wq = dt("scr_wq", [D, D], BF16, kind="Internal")
    scr_h2 = dt("scr_h2", [N, D], BF16, kind="Internal")
    scr_mo = dt("scr_mo", [D, N], BF16, kind="Internal")
    scr_z = dt("scr_z", [N, D], BF16, kind="Internal")
    scr_g = dt("scr_g", [INTER, N], BF16, kind="Internal")
    scr_f = dt("scr_f", [D, N], BF16, kind="Internal")
    scr_w1q = dt("scr_w1q", [INTER, D], BF16, kind="Internal")
    scr_w2q = dt("scr_w2q", [D, INTER], BF16, kind="Internal")
    scr_wg = dt("scr_wg", [D, INTER], BF16, kind="Internal")
    scr_wu = dt("scr_wu", [D, INTER], BF16, kind="Internal")
    scr_wd = dt("scr_wd", [INTER, D], BF16, kind="Internal")
    cc_in = dt("cc_in", [P, H * P], F32, kind="Internal")
    cc_out = dt("cc_out", [NCORES * P, H * P], F32, kind="Internal",
                addr_space="Shared")

    DBG = {}
    for name, shape, ddt in (
        ("d_y0", [N, D], F32), ("d_h1", [N, D], BF16), ("d_kfm", [D, N], BF16),
        ("d_ds", [P, H * P], F32), ("d_sst", [P, H * P], F32),
        ("d_o", [N, D], BF16), ("d_ao", [D, N], BF16), ("d_y1", [N, D], F32),
        ("d_h2", [N, D], BF16), ("d_mo", [D, N], BF16), ("d_x1", [N, D], F32),
        ("d_z", [N, D], BF16), ("d_g", [INTER, N], BF16), ("d_f", [D, N], BF16),
        ("d_w1q", [INTER, D], BF16), ("d_w2q", [D, INTER], BF16),
        ("d_scols", [P, 16], F32), ("d_a1c", [P, 176], F32),
    ):
        if name in dbg:
            DBG[name] = dt(name, shape, ddt, kind="ExternalOutput")

    with ExitStack() as stack:
        E = stack.enter_context
        b = Ctx(nc, stack)

        class _LD(dict):
            def __missing__(self, k):
                try:
                    if blk.on:
                        raise KeyError(k)
                except NameError:
                    raise KeyError(k)
                return 0

        LD = _LD()
        wp1_dve_count = {}

        Sf = E(nc.sbuf_tensor("Sf", [P, 16, 1024], F32))
        S2 = E(nc.sbuf_tensor("S2", [P, 8, N], BF16))
        S3 = E(nc.sbuf_tensor("S3", [P, 8, N], BF16))
        S4 = E(nc.sbuf_tensor("S4", [P, 8, N], BF16))
        S5 = E(nc.sbuf_tensor("S5", [P, 16, 1024], BF16))
        mask_sb = E(nc.sbuf_tensor("mask_sb", [P, P], BF16))
        idf_sb = E(nc.sbuf_tensor("idf_sb", [P, P], F32))
        onesr_sb = E(nc.sbuf_tensor("onesr_sb", [1, P], BF16))
        onesf_sb = E(nc.sbuf_tensor("onesf_sb", [P, P], F32))
        owc_sb = E(nc.sbuf_tensor("owc_sb", [P, 1], F32))
        mcol_sb = E(nc.sbuf_tensor("mcol_sb", [P, NCORES], F32))
        stc = E(nc.sbuf_tensor("stc", [P, 176], F32))
        mf0 = E(nc.sbuf_tensor("mf0", [P, 1024], F32)).ap()
        mf1 = E(nc.sbuf_tensor("mf1", [P, 1024], F32)).ap()
        mb0 = E(nc.sbuf_tensor("mb0", [P, 1024], BF16)).ap()
        mb1 = E(nc.sbuf_tensor("mb1", [P, 1024], BF16)).ap()
        trow = E(nc.sbuf_tensor("trow", [16, P], F32)).ap()
        ps = [E(nc.psum_tensor(f"ps{i}", [P, 512], F32)) for i in range(8)]

        class _Blk:
            def __init__(self, blk):
                self.blk, self.on = blk, True

            def _mk(self, n):
                eng = getattr(self.blk, n)
                def deco(fn):
                    if self.on:
                        eng(fn)
                return deco

            @property
            def sync(self): return self._mk("sync")
            @property
            def scalar(self): return self._mk("scalar")
            @property
            def tensor(self): return self._mk("tensor")
            @property
            def vector(self): return self._mk("vector")
            @property
            def gpsimd(self): return self._mk("gpsimd")

        _real_blk = E(nc.Block())
        blk = _Blk(_real_blk)

        def mark(name):
            if stop == name:
                blk.on = False

        _cp = [16]

        def cols(n):
            a = _cp[0]
            _cp[0] += n
            assert _cp[0] <= 176
            return stc[:, a:a + n]

        def col_reset():
            _cp[0] = 16

        c_wsum = stc[:, 0:2]
        c_wtmp = stc[:, 2:4]
        c_wtot = stc[:, 4:6]
        c_th = stc[:, 6:8]
        c_nth = stc[:, 8:10]
        c_s = stc[:, 10:12]
        c_junk = stc[:, 12:13]

        def verify(ve, *aps):
            ve.memset(c_junk, 0.0)
            last = None
            for a in aps:
                r = int(a.shape[0])
                last = ve.tensor_reduce(c_junk[0:r, :], a, axis=AX.X, op=ALU.add)
            return last

        def sfb(lo, hi):
            return Sf[:, lo:hi, :].bitcast(BF16).rearrange("p a d -> p (a d)")

        x_stream = [Sf[:, 0, :], Sf[:, 1, :]]
        wpb = [Sf[:, 2, :], Sf[:, 3, :]]
        anw_sb = Sf[:, 6, :]
        h1_stage = [Sf[:, 7, :], Sf[:, 8, :]]
        wA = sfb(4, 8).rearrange("p (a d) -> p a d", d=1024)
        wB = sfb(8, 12).rearrange("p (a d) -> p a d", d=1024)
        srun = Sf[:, 0, :]
        sstart = Sf[:, 1, :]
        sstart_b = sfb(4, 5)[:, 0:1024]
        sq_buf = sfb(5, 6)
        wq_t1 = sfb(7, 8)
        Gbuf = [Sf[:, 8, :], Sf[:, 9, :]]
        ktm_buf = [sfb(10, 11)[:, 0:1024], sfb(10, 11)[:, 1024:2048]]
        sp_buf = [sfb(11, 12)[:, 0:1024], sfb(11, 12)[:, 1024:2048]]
        spo_buf = [sfb(12, 13)[:, 0:1024], sfb(12, 13)[:, 1024:2048]]
        aT4 = [sfb(13, 14)[:, 0:512].rearrange("p (a d) -> p a d", d=P),
               sfb(13, 14)[:, 512:1024].rearrange("p (a d) -> p a d", d=P)]
        rstd_o = Sf[:, 15, 0:128]
        resid = Sf

        h1_fm = S2
        o_bf = S2[:, :, :].rearrange("p h (c v) -> p h c v", c=NCH)
        h2_fm = S2
        zint_fm = S2
        gq_sb = S2[:, :, :].rearrange("p h (a t) -> p (h a) t", a=4)
        f_tm = S2[:, :, :].rearrange("p h (a d) -> p (h a) d", a=2)

        q_fm = S3
        onrm_fm = S3
        y2acc = S3
        f_fm = S3
        mnw_bf = S3[:, 0, 0:1024]
        junk_b1 = S3[:, 1, 0:1024]

        k_fm = S4
        w1T = S4
        w2T_buf = [S4[:, 0, 0:1024], S4[:, 0, 1024:2048],
                   S4[:, 2, 0:1024], S4[:, 2, 1024:2048]]
        m2_sb = S4[:, 1, :]
        junk_c = S4[:, 2, 0:1024]

        def wg_view(par):
            return S4[:, par * 4:par * 4 + 2, :].rearrange("p a t -> p (a t)").rearrange(
                "p (a o) -> p a o", o=512)

        def wu_view(par):
            return S4[:, par * 4 + 2:par * 4 + 4, :].rearrange("p a t -> p (a t)").rearrange(
                "p (a o) -> p a o", o=512)

        v_tm = S5[:, :, :]
        wo_sb = S5[:, 0:8, :]
        flipb = [S5[:, 8, :], S5[:, 9, :]]
        wd_view = [S5[:, 0:4, :], S5[:, 4:8, :]]
        sg_bufs = [S5[:, 10, 0:512], S5[:, 10, 512:1024]]
        g_tile = S5[:, 11:13, :].rearrange("p a (c t) -> p (a c) t", c=2)
        m1_sb = S5[:, 0:2, :].rearrange("p a d -> p (a d)")
        gmax = S5[:, 2:6, :].bitcast(F32).rearrange("p a d -> p (a d)")
        g2row = [S5[:, 6:8, :].rearrange("p a d -> p (a d)"),
                 S5[:, 8:10, :].rearrange("p a d -> p (a d)")]
        mrow_f = S5[:, 10:14, :].bitcast(F32).rearrange("p a d -> p (a d)")
        gmin = S5[:, 10:14, :].bitcast(F32).rearrange("p a d -> p (a d)")
        mrow_b = S5[:, 14:16, :].rearrange("p a d -> p (a d)")
        g2rb = [S5[:, 6, 0:512], S5[:, 6, 512:1024],
                S5[:, 0, 0:512], S5[:, 0, 512:1024]]
        fstage = [S5[:, 8, :], S5[:, 9, :]]

        def wblk(src, i5):
            if i5 < 32:
                return src[i5 * P:(i5 + 1) * P, :]
            i6 = i5 - 32
            rb, cb = i6 // 4, i6 % 4
            return src[rb * P:(rb + 1) * P, cb * 1024:(cb + 1) * 1024]

        ewp = [Sf[:, 14, :], Sf[:, 15, :]]

        @blk.sync
        def _(sy):
            for ddst, dsrc in ((mask_sb[:, :], mask_h[:, :]), (idf_sb[:, :], idf_h[:, :]),
                               (onesr_sb[:, :], onesr_h[:, :]), (onesf_sb[:, :], onesf_h[:, :]),
                               (owc_sb[:, :], owc_h[:, :]), (mcol_sb[:, :], mcol_h[:, :]),
                               (anw_sb, anw_h[:, :])):
                c = b.dma("dCONST", sy, ddst, dsrc)
            LD["consts"] = ("dCONST", c)
            b.dma("dWPE", sy, wpb[0], wblk(w1_h, 0))
            b.dma("dWPE", sy, wpb[1], wblk(w1_h, 1))
            b.dma("dWPE", sy, ewp[0], wblk(w2_h, 32))
            b.dma("dWPE", sy, ewp[1], wblk(w2_h, 33))

        @blk.scalar
        def _(ac):
            for k in range(3):
                b.dma("dWRM", ac, scr_warm[k:k + 1, :], stc[0:1, 0:16])

        @blk.sync
        def _(sy):
            for k in range(3, 6):
                b.dma("dWRMS", sy, scr_warm[k:k + 1, :], stc[0:1, 0:16])

        c_sum = cols(16); c_ssq = cols(16); c_m = cols(16); c_var = cols(16)
        c_rho1 = cols(16); c_rho = cols(16); c_nmr1 = cols(16); c_nmr = cols(16)
        c_tmp = cols(16)

        a1_dve0 = b.c["dve"]

        @blk.sync
        def _(sy):
            for t in range(NT):
                b.w(sy, "dve", a1_dve0 + (t - 2) + 1 if t >= 2 else 0)
                LD[f"a1x{t}"] = (f"dA1X{t % 2}", b.dma(f"dA1X{t % 2}", sy, x_stream[t % 2],
                                                       x_h[t * P:(t + 1) * P, :]))

        @blk.vector
        def _(ve):
            for t in range(NT):
                b.wd(ve, *LD[f"a1x{t}"])
                ve.reduce_sum(c_sum[:, t:t + 1], x_stream[t % 2], axis=AX.X)
                i = ve.scalar_tensor_tensor(mb0[:, :], x_stream[t % 2], 1.0, x_stream[t % 2],
                                            ALU.mult, ALU.mult, accum_out=c_ssq[:, t:t + 1])
                b.inc("dve", i)
            ve.tensor_scalar(c_m, c_sum, 1.0 / D, None, ALU.mult)
            ve.tensor_scalar(c_tmp, c_ssq, 1.0 / D, None, ALU.mult)
            ve.tensor_mul(c_var, c_m, c_m)
            ve.memset(c_junk, 0.0)
            ve.tensor_sub(c_var, c_tmp, c_var)
            ve.memset(c_junk, 0.0)
            ve.tensor_scalar(c_tmp, c_var, EPS, None, ALU.add)
            LD["a1vpe"] = b.inc("dve", verify(ve, c_tmp))

        @blk.scalar
        def _(ac):
            b.w(ac, "dve", LD["a1vpe"])
            ac.activation(c_rho1, c_tmp, AF.Ln)
            i = ac.activation(c_rho1, c_rho1, AF.Exp, scale=-0.5)
            LD["a1rho1"] = b.inc("act", i)

        @blk.vector
        def _(ve):
            b.w(ve, "act", LD["a1rho1"])
            i = ve.reciprocal(c_rho, c_tmp)
            b.inc("dve", i)
            b.w(ve, "dve", b.c["dve"])
            ve.tensor_mul(c_rho, c_var, c_rho)
            ve.memset(c_junk, 0.0)
            ve.tensor_scalar(c_rho, c_rho, EPS, None, ALU.add)
            LD["a1r2"] = b.inc("dve", verify(ve, c_rho))

        @blk.scalar
        def _(ac):
            b.w(ac, "dve", LD["a1r2"])
            ac.activation(c_rho, c_rho, AF.Ln)
            i = ac.activation(c_rho, c_rho, AF.Exp, scale=-0.5)
            LD["a1rho2"] = b.inc("act", i)

        @blk.vector
        def _(ve):
            b.w(ve, "act", LD["a1rho2"])
            ve.tensor_mul(c_rho, c_rho1, c_rho)
            ve.scalar_tensor_tensor(c_nmr1, c_m, -1.0, c_rho1, ALU.mult, ALU.mult)
            ve.scalar_tensor_tensor(c_nmr, c_m, -1.0, c_rho, ALU.mult, ALU.mult)
            LD["a1cols"] = b.inc("dve", verify(ve, c_rho, c_nmr1, c_nmr))

        if "d_a1c" in DBG:
            @blk.sync
            def _(sy):
                b.w(sy, "dve", LD["a1cols"])
                b.dma("dDBG", sy, DBG["d_a1c"][:, :], stc[:, :])

        a1_act0 = b.c["act"]
        a1_hv0 = b.c["dve"]

        @blk.sync
        def _(sy):
            for t in range(NT):
                if t >= 2:
                    b.w(sy, "act", a1_act0 + (t - 2) + 1)
                else:
                    b.w(sy, "dve", a1_dve0 + 15 + t)
                LD[f"a1x2{t}"] = (f"dA1Y{t % 2}", b.dma(f"dA1Y{t % 2}", sy, x_stream[t % 2],
                                                        x_h[t * P:(t + 1) * P, :]))

        @blk.scalar
        def _(ac):
            b.w(ac, "dve", LD["a1cols"])
            for t in range(NT):
                b.wd(ac, *LD[f"a1x2{t}"])
                if t >= 2:
                    b.w(ac, "dve", a1_hv0 + (t - 2) + 1)
                else:
                    ac.copy(mb0 if t == 0 else mb1, x_stream[t % 2])
                b.inc("act", ac.activation(h1_stage[t % 2], x_stream[t % 2], AF.Identity,
                                           scale=c_rho[:, t:t + 1], bias=c_nmr[:, t:t + 1]))
                if t >= 1:
                    b.w(ac, "dve", a1_hv0 + (t - 1) + 1)
                    src = mb0[:, :] if (t - 1) % 2 == 0 else mb1[:, :]
                    LD[f"a1h1d{t - 1}"] = (f"dH1{(t - 1) % 2}",
                                           b.dma(f"dH1{(t - 1) % 2}", ac,
                                                 scr_h1[(t - 1) * P:t * P, :], src))
            b.w(ac, "dve", a1_hv0 + NT)
            b.dma("dH11", ac, scr_h1[(NT - 1) * P:NT * P, :], mb1[:, :])
            LD[f"a1h1d{NT - 1}"] = ("dH11", b.dma("dH11", ac,
                                                  scr_h1[(NT - 1) * P:NT * P, :], mb1[:, :]))

        @blk.vector
        def _(ve):
            b.wd(ve, *LD["consts"])
            for t in range(NT):
                b.w(ve, "act", a1_act0 + t + 1)
                if t >= 2:
                    b.wd(ve, *LD[f"a1h1d{t - 2}"])
                dst = mb0[:, :] if t % 2 == 0 else mb1[:, :]
                i = ve.tensor_mul(dst, h1_stage[t % 2], anw_sb)
                b.inc("dve", i)
        LD["a1_dve"] = b.c["dve"]
        LD["a1_act"] = b.c["act"]

        @blk.gpsimd
        def _(gp):
            b.w(gp, "dve", LD["a1_dve"])
            b.w(gp, "act", LD["a1_act"])
            LD["wk"] = ("dWK", b.dma("dWK", gp, wA, wk_h.ap().rearrange("(a p) o -> p a o", p=P)))
            LD["wv"] = ("dWV", b.dma("dWV", gp, wB, wv_h.ap().rearrange("(a p) o -> p a o", p=P)))
            LD["wo_pre"] = ("dWOP", b.dma("dWOP", gp, scr_wo[:, :], wo_h[:, :]))
            LD["wq_pre"] = ("dWQP", b.dma("dWQP", gp, scr_wq[:, :], wq_h[:, :]))

        if "d_h1" in DBG:
            @blk.sync
            def _(sy):
                b.wd(sy, "dH10"); b.wd(sy, "dH11")
                b.dma("dDBG", sy, DBG["d_h1"][:, :], scr_h1[:, :])

        @blk.sync
        def _(sy):
            b.wd(sy, "dH10"); b.wd(sy, "dH11")
            for rep in range(2):
                for fc in range(8):
                    c = b.dma("dH1F", sy, h1_fm[:, fc, :],
                              scr_h1[:, fc * P:(fc + 1) * P], transpose=True)
            LD["h1fm"] = ("dH1F", c)

        mark("a2")
        a3a_act0 = b.c["act"]

        @blk.tensor
        def _(te):
            if "a3a" in skip:
                return
            b.wd(te, *LD["wk"])
            b.wd(te, *LD["wv"])
            b.wd(te, *LD["h1fm"])
            te.matmul(ps[0][:, :], wA[:, 7, 0:P], h1_fm[:, 7, 0:512],
                      start=True, stop=True)
            te.matmul(ps[1][:, :], h1_fm[:, 7, 0:P], wB[:, 7, 0:512],
                      start=True, stop=True)
            r = 0
            for j in range(64):
                if j >= 4:
                    b.w(te, "act", a3a_act0 + (j - 4) + 1)
                if j < 32:
                    h, nb = j // 4, j % 4
                    for kc in range(8):
                        i = te.matmul(ps[r][:, :], wA[:, kc, h * P:(h + 1) * P],
                                      h1_fm[:, kc, nb * 512:(nb + 1) * 512],
                                      start=(kc == 0), stop=(kc == 7))
                else:
                    tc, hf = (j - 32) // 2, (j - 32) % 2
                    for kc in range(8):
                        i = te.matmul(ps[r][:, :], h1_fm[:, kc, tc * P:(tc + 1) * P],
                                      wB[:, kc, hf * 512:(hf + 1) * 512],
                                      start=(kc == 0), stop=(kc == 7))
                b.inc("pe", i)
                LD[f"a3m{j}"] = b.c["pe"]
                r = (r + 1) % 4
        LD["a3a_pe"] = b.c["pe"]

        @blk.scalar
        def _(ac):
            if "a3a" in skip:
                return
            r = 0
            for j in range(64):
                b.w(ac, "pe", LD[f"a3m{j}"])
                if j < 32:
                    h, nb = j // 4, j % 4
                    i = ac.copy(k_fm[:, h, nb * 512:(nb + 1) * 512], ps[r][:, :])
                else:
                    tc, hf = (j - 32) // 2, (j - 32) % 2
                    i = ac.copy(v_tm[:, tc, hf * 512:(hf + 1) * 512], ps[r][:, :])
                b.inc("act", i)
                r = (r + 1) % 4
            LD["a3a_act"] = b.c["act"]
            b.dma("dSCRK", ac, scr_k.ap().rearrange("(h p) t -> p h t", p=P), k_fm[:, :, :])
            LD["scrk"] = ("dSCRK", b.dma("dSCRK", ac, scr_k.ap().rearrange("(h p) t -> p h t", p=P),
                                         k_fm[:, :, :]))
        bank_last = {i: None for i in range(8)}
        if "a3a" not in skip:
            for i in range(4):
                bank_last[i] = ("act", LD["a3a_act"])

        a4_pe0 = b.c["pe"]
        a4_dve1 = None

        @blk.sync
        def _(sy):
            if "a4" in skip:
                return
            b.wd(sy, *LD["scrk"])
            for c in range(NCH):
                b.w(sy, "pe", a4_pe0 + (c - 2) + 1 if c >= 2 else 0)
                if c < 2:
                    b.dma(f"dKT{c % 2}", sy, ktm_buf[c % 2],
                          scr_k[:, c * P:(c + 1) * P], transpose=True)
                LD[f"a4k{c}"] = (f"dKT{c % 2}", b.dma(f"dKT{c % 2}", sy, ktm_buf[c % 2],
                                                      scr_k[:, c * P:(c + 1) * P], transpose=True))

        @blk.vector
        def _(ve):
            if "a4" in skip:
                return
            b.w(ve, "act", LD["a1_act"])
            i = ve.memset(srun, 0.0)
            LD["srun0"] = b.inc("dve", i)

        def wp1_wait(i5):
            if i5 % 2 == 0:
                n = i5 // 2 if i5 <= 30 else i5 // 2 - 1
                return ("dWP10", 16 * n)
            n = (i5 - 1) // 2 if i5 <= 31 else (i5 - 1) // 2 - 1
            return ("dWP11", 16 * n)

        @blk.vector
        def _(ve):
            if "wprep" in skip:
                return
            for i5 in range(32):
                if i5 < 2:
                    b.wd(ve, "dWPE", 64)
                else:
                    b.wd(ve, *wp1_wait(i5))
                dst = c_wsum[:, 0:1] if i5 == 0 else c_wtmp[:, 0:1]
                b.inc("dve", ve.tensor_reduce(dst, wpb[i5 % 2], axis=AX.X, op=ALU.add,
                                              apply_absolute_value=True))
                ve.memset(c_junk, 0.0)
                ve.tensor_reduce(dst, wpb[i5 % 2], axis=AX.X, op=ALU.add,
                                 apply_absolute_value=True)
                if i5 > 0:
                    ve.memset(c_junk, 0.0)
                    b.inc("dve", ve.tensor_add(c_wsum[:, 0:1], c_wsum[:, 0:1], c_wtmp[:, 0:1]))
                wp1_dve_count[i5] = b.c["dve"]
        a4_dve1 = b.c["dve"]

        @blk.tensor
        def _(te):
            if "a4" in skip:
                return
            b.w(te, "dve", LD["srun0"])
            for c in range(NCH):
                b.wd(te, *LD[f"a4k{c}"])
                if c == 0:
                    b.wd(te, *LD["a4k1"])
                b.w(te, "dve", a4_dve1 + (c - 2) + 1 if c >= 2 else 0)
                pr = (ps[4], ps[5]) if c % 2 == 0 else (ps[6], ps[7])
                for h in range(H):
                    i = te.matmul(pr[h // 4][:, (h % 4) * P:(h % 4 + 1) * P],
                                  ktm_buf[c % 2][:, h * P:(h + 1) * P],
                                  v_tm[:, c, h * P:(h + 1) * P], start=True, stop=True)
                b.inc("pe", i)
                LD[f"a4m{c}"] = b.c["pe"]

        a4_act0 = b.c["act"]

        @blk.vector
        def _(ve):
            if "a4" in skip:
                return
            for c in range(NCH):
                b.w(ve, "pe", LD[f"a4m{c}"])
                b.w(ve, "act", (a4_act0 + (2 if c == 1 else c + 1)) if c >= 1 else 0)
                pr = (ps[4], ps[5]) if c % 2 == 0 else (ps[6], ps[7])
                ve.tensor_add(srun[:, 0:512], srun[:, 0:512], pr[0][:, :])
                i = ve.tensor_add(srun[:, 512:1024], srun[:, 512:1024], pr[1][:, :])
                b.inc("dve", i)
                LD[f"a4s{c}"] = b.c["dve"]
        LD["a4_dve"] = b.c["dve"]

        @blk.vector
        def _(ve):
            if "wprep" in skip:
                return
            for i5 in range(32, 64):
                if i5 < 34:
                    b.wd(ve, "dWPE", 64)
                    src = ewp[i5 - 32]
                else:
                    b.wd(ve, *wp1_wait(i5))
                    src = wpb[i5 % 2]
                dst = c_wsum[:, 1:2] if i5 == 32 else c_wtmp[:, 1:2]
                b.inc("dve", ve.tensor_reduce(dst, src, axis=AX.X, op=ALU.add,
                                              apply_absolute_value=True))
                ve.memset(c_junk, 0.0)
                ve.tensor_reduce(dst, src, axis=AX.X, op=ALU.add,
                                 apply_absolute_value=True)
                if i5 > 32:
                    ve.memset(c_junk, 0.0)
                    b.inc("dve", ve.tensor_add(c_wsum[:, 1:2], c_wsum[:, 1:2], c_wtmp[:, 1:2]))
                wp1_dve_count[i5] = b.c["dve"]
            ve.memset(c_junk, 0.0)
            LD["wsum2"] = b.inc("dve", verify(ve, c_wsum))
        LD["wsum"] = b.c["dve"]

        @blk.scalar
        def _(ac):
            if "a4" in skip:
                return
            for c in range(NCH):
                b.w(ac, "dve", LD[f"a4s{c}"])
                if c >= 2:
                    b.wd(ac, *LD[f"a4w{c - 2}"])
                b.inc("act", ac.copy(spo_buf[c % 2], srun))
                if c == 0:
                    LD["spb1"] = b.inc("act", ac.copy(sp_buf[1], srun))
                if c >= 1:
                    cp = c - 1
                    LD[f"a4w{cp}"] = (f"dSP{cp % 2}",
                                      b.dma(f"dSP{cp % 2}", ac,
                                            scr_sp[cp * P:(cp + 1) * P, :], spo_buf[cp % 2]))
            ac.copy(sstart[:, 0:512], srun[:, 0:512])
            cp = NCH - 1
            LD[f"a4w{cp}"] = (f"dSP{cp % 2}",
                              b.dma(f"dSP{cp % 2}", ac,
                                    scr_sp[cp * P:(cp + 1) * P, :], spo_buf[cp % 2]))
            LD["ccin"] = ("dCCIN", b.dma("dCCIN", ac, cc_in[:, :], srun))
        for i in range(4, 8):
            bank_last[i] = ("dve", LD["a4_dve"])


        @blk.gpsimd
        def _(gp):
            if "wprep" not in skip:
                for i5 in range(64):
                    if i5 in (0, 1, 32, 33):
                        continue
                    src = w1_h if i5 < 32 else w2_h
                    gate = i5 - 4 if i5 in (34, 35) else i5 - 2
                    b.w(gp, "dve", wp1_dve_count[gate])
                    LD[f"wp1l{i5}"] = (f"dWP1{i5 % 2}", b.dma(f"dWP1{i5 % 2}", gp,
                                                              wpb[i5 % 2], wblk(src, i5)))
            if "cc" not in skip and "a4" not in skip:
                b.wd(gp, *LD["ccin"])
                i = gp.collective_compute("AllGather", ALU.bypass,
                                          replica_groups=[list(range(NCORES))],
                                          ins=[cc_in.ap().opt()], outs=[cc_out.ap().opt()])
                LD["cc"] = b.inc("ccs", i)
            pass

        @blk.sync
        def _(sy):
            if "a3b" in skip:
                return
            b.wd(sy, *LD["wq_pre"])
            b.w(sy, "pe", LD["a3a_pe"] if "a3a" not in skip else 0)
            LD["wqsb"] = ("dWQS", b.dma("dWQS", sy, wA,
                                        scr_wq.ap().rearrange("(a p) o -> p a o", p=P)))

        a3b_act0 = b.c["act"]

        @blk.tensor
        def _(te):
            if "a3b" in skip:
                return
            b.wd(te, *LD["wqsb"])
            te.matmul(ps[0][:, :], wA[:, 7, 0:P], h1_fm[:, 7, 0:512],
                      start=True, stop=True)
            r = 0
            for j in range(32):
                h, nb = j // 4, j % 4
                if j >= 4:
                    b.w(te, "act", a3b_act0 + (j - 4) + 1)
                for kc in range(8):
                    i = te.matmul(ps[r][:, :], wA[:, kc, h * P:(h + 1) * P],
                                  h1_fm[:, kc, nb * 512:(nb + 1) * 512],
                                  start=(kc == 0), stop=(kc == 7))
                b.inc("pe", i)
                LD[f"a3q{j}"] = b.c["pe"]
                r = (r + 1) % 4
        LD["a3b_pe"] = b.c["pe"]

        @blk.scalar
        def _(ac):
            if "a3b" in skip:
                return
            r = 0
            for j in range(32):
                h, nb = j // 4, j % 4
                b.w(ac, "pe", LD[f"a3q{j}"])
                b.inc("act", ac.copy(q_fm[:, h, nb * 512:(nb + 1) * 512], ps[r][:, :]))
                r = (r + 1) % 4
            LD["a3b_act"] = b.c["act"]
        if "a3b" not in skip:
            for i in range(4):
                bank_last[i] = ("act", LD["a3b_act"])

        @blk.tensor
        def _(te):
            if "wprep" in skip:
                return
            b.w(te, "dve", LD["wsum"])
            b.w(te, bank_last[4][0], bank_last[4][1])
            i = te.matmul(ps[4][:, 0:2], onesf_sb[:, :], c_wsum, start=True, stop=True)
            b.inc("pe", i)
            LD["wtotmm"] = b.c["pe"]

        @blk.vector
        def _(ve):
            if "wprep" in skip:
                return
            b.w(ve, "pe", LD["wtotmm"])
            i = ve.tensor_copy(c_wtot, ps[4][:, 0:2])
            b.inc("dve", i)
            b.w(ve, "dve", b.c["dve"])
            b.inc("dve", ve.tensor_scalar(c_s, c_wtot, 1.0 / (INTER * D), None, ALU.mult))
            b.w(ve, "dve", b.c["dve"])
            b.inc("dve", ve.tensor_scalar(c_s, c_s, EPS, None, ALU.max))
            b.w(ve, "dve", b.c["dve"])
            b.inc("dve", ve.tensor_scalar(c_th, c_s, 0.5, None, ALU.mult))
            i = ve.tensor_scalar(c_nth, c_s, -0.5, None, ALU.mult)
            LD["wth"] = b.inc("dve", i)
        if "wprep" not in skip:
            bank_last[4] = ("dve", LD["wth"])

        if "d_scols" in DBG:
            @blk.sync
            def _(sy):
                b.w(sy, "dve", LD["wth"])
                b.dma("dDBG", sy, DBG["d_scols"][:, :], stc[:, 0:16])
        if "d_kfm" in DBG:
            @blk.sync
            def _(sy):
                b.wd(sy, *LD["scrk"])
                b.dma("dDBG", sy, DBG["d_kfm"][:, :], scr_k[:, :])
        if "d_ds" in DBG:
            @blk.sync
            def _(sy):
                b.wd(sy, *LD["ccin"])
                b.dma("dDBG", sy, DBG["d_ds"][:, :], cc_in[:, :])

        mark("pre_a6")
        a6_pe0 = b.c["pe"]
        a6_dve0 = b.c["dve"]
        a6_act0 = b.c["act"]

        @blk.sync
        def _(sy):
            for c in range(2, NCH):
                b.w(sy, "pe", a6_pe0 + 3 * (c - 2) + 3 if c >= 3 else 0)
                b.wd(sy, *LD[f"a4w{c - 1}"])
                b.wd(sy, *LD[f"a4w{c}"])
                LD[f"a6p{c}"] = (f"dA6P{c % 2}", b.dma(f"dA6P{c % 2}", sy, sp_buf[c % 2],
                                                       scr_sp[(c - 1) * P:c * P, :]))

        @blk.tensor
        def _(te):
            b.w(te, "act", LD["a3b_act"])
            for c in range(NCH):
                if c == 1:
                    b.w(te, "act", LD["spb1"])
                elif c >= 2:
                    b.wd(te, *LD[f"a6p{c}"])
                for half in range(2):
                    b.w(te, "dve", a6_dve0 + 2 * (c - 1) + half + 1 if c >= 1 else 0)
                    if c == 0 and bank_last[6 + half]:
                        b.w(te, bank_last[6 + half][0], bank_last[6 + half][1])
                    for hh in range(4):
                        h = half * 4 + hh
                        i = te.matmul(ps[6 + half][:, hh * P:(hh + 1) * P],
                                      k_fm[:, h, c * P:(c + 1) * P],
                                      q_fm[:, h, c * P:(c + 1) * P], start=True, stop=True)
                    b.inc("pe", i)
                b.w(te, "act", a6_act0 + 2 * (c - 2) + 2 if c >= 2 else 0)
                if c < 2:
                    for bb in (c % 2, 2 + c % 2):
                        if bank_last[bb]:
                            b.w(te, bank_last[bb][0], bank_last[bb][1])
                b.w(te, "dve", a6_dve0 + 2 * c + 2)
                for half in range(2):
                    ob = ps[half * 2 + (c % 2)]
                    for hh in range(4):
                        h = half * 4 + hh
                        i = te.matmul(ob[:, hh * P:(hh + 1) * P], aT4[half][:, hh, :],
                                      v_tm[:, c, h * P:(h + 1) * P],
                                      start=True, stop=(c == 0))
                        if c >= 1:
                            i = te.matmul(ob[:, hh * P:(hh + 1) * P],
                                          q_fm[:, h, c * P:(c + 1) * P],
                                          sp_buf[c % 2][:, h * P:(h + 1) * P],
                                          start=False, stop=True)
                b.inc("pe", i)
                LD[f"a6o{c}"] = b.c["pe"]
        LD["a6_pe"] = b.c["pe"]

        @blk.vector
        def _(ve):
            for c in range(NCH):
                for half in range(2):
                    b.w(ve, "pe", a6_pe0 + 3 * c + half + 1)
                    i = ve.tensor_tensor(aT4[half][:, :, :],
                                         ps[6 + half][:, :].rearrange("p (a d) -> p a d", d=P),
                                         mask_sb[:, :].unsqueeze(1).broadcast_to([P, 4, P]),
                                         ALU.mult)
                    b.inc("dve", i)
        LD["a6_dve"] = b.c["dve"]

        @blk.scalar
        def _(ac):
            for c in range(NCH):
                b.w(ac, "pe", LD[f"a6o{c}"])
                for half in range(2):
                    ob = ps[half * 2 + (c % 2)]
                    i = ac.copy(o_bf[:, half * 4:(half + 1) * 4, c, :],
                                ob[:, :].rearrange("p (a d) -> p a d", d=P))
                    b.inc("act", i)
            LD["a6_act"] = b.c["act"]
        for i in range(4):
            bank_last[i] = ("act", LD["a6_act"])
        bank_last[6] = ("dve", LD["a6_dve"])
        bank_last[7] = ("dve", LD["a6_dve"])

        a7_dve0 = b.c["dve"]

        @blk.sync
        def _(sy):
            b.w(sy, "ccs", LD["cc"])
            for r_ in range(NCORES):
                b.w(sy, "dve", a7_dve0 + (r_ - 2) + 2 if r_ >= 2 else 0)
                LD[f"a7l{r_}"] = (f"dG{r_ % 2}", b.dma(f"dG{r_ % 2}", sy, Gbuf[r_ % 2],
                                                       cc_out[r_ * P:(r_ + 1) * P, :]))

        @blk.vector
        def _(ve):
            b.w(ve, "pe", LD["a3b_pe"])
            i = ve.memset(sstart, 0.0)
            b.inc("dve", i)
            for r_ in range(NCORES):
                b.wd(ve, *LD[f"a7l{r_}"])
                ve.tensor_reduce(c_junk, Gbuf[r_ % 2], axis=AX.X, op=ALU.add)
                ve.memset(c_junk, 0.0)
                i = ve.scalar_tensor_tensor(sstart, Gbuf[r_ % 2], mcol_sb[:, r_:r_ + 1],
                                            sstart, ALU.mult, ALU.add)
                b.inc("dve", i)
            LD["a7ss"] = b.c["dve"]

        @blk.scalar
        def _(ac):
            b.w(ac, "dve", LD["a7ss"])
            b.w(ac, "pe", LD["a3b_pe"])
            i = ac.copy(sstart_b, sstart)
            LD["a7ssb"] = b.inc("act", i)

        if "d_sst" in DBG:
            @blk.sync
            def _(sy):
                b.w(sy, "dve", LD["a7ss"])
                b.dma("dDBG", sy, DBG["d_sst"][:, :], sstart)

        a7_add0 = b.c["dve"]

        @blk.tensor
        def _(te):
            b.w(te, "act", LD["a7ssb"])
            b.w(te, "act", LD["a6_act"])
            for j in range(32):
                h, cg = j // 4, j % 4
                bk = ps[4 + j % 2]
                if j >= 2:
                    b.w(te, "dve", a7_add0 + (j - 2) + 1)
                elif bank_last[4 + j % 2]:
                    b.w(te, bank_last[4 + j % 2][0], bank_last[4 + j % 2][1])
                for cc2 in range(4):
                    c = cg * 4 + cc2
                    i = te.matmul(bk[:, cc2 * P:(cc2 + 1) * P],
                                  q_fm[:, h, c * P:(c + 1) * P],
                                  sstart_b[:, h * P:(h + 1) * P], start=True, stop=True)
                b.inc("pe", i)
                LD[f"a7m{j}"] = b.c["pe"]
        LD["a7_pe"] = b.c["pe"]

        @blk.vector
        def _(ve):
            for j in range(32):
                h, cg = j // 4, j % 4
                b.w(ve, "pe", LD[f"a7m{j}"])
                i = ve.tensor_tensor(o_bf[:, h, cg * 4:(cg + 1) * 4, :],
                                     o_bf[:, h, cg * 4:(cg + 1) * 4, :],
                                     ps[4 + j % 2][:, :].rearrange("p (a d) -> p a d", d=P),
                                     ALU.add)
                b.inc("dve", i)
            LD["a7add"] = b.c["dve"]
        bank_last[4] = ("dve", LD["a7add"])
        bank_last[5] = ("dve", LD["a7add"])

        a7_rd0 = b.c["dve"]

        @blk.scalar
        def _(ac):
            b.w(ac, "dve", LD["a7add"])
            for h in range(H):
                b.w(ac, "dve", a7_rd0 + (h - 1) + 1 if h >= 1 else 0)
                i = ac.activation(sq_buf[:, :], S2[:, h, :], AF.Square)
                b.inc("act", i)
                LD[f"a7sq{h}"] = b.c["act"]

        @blk.vector
        def _(ve):
            for h in range(H):
                b.w(ve, "act", LD[f"a7sq{h}"])
                i = ve.tensor_reduce(rstd_o[:, h * 16:(h + 1) * 16].unsqueeze(2),
                                     sq_buf[:, :].rearrange("p (a d) -> p a d", d=P),
                                     axis=AX.X, op=ALU.add)
                b.inc("dve", i)
            ve.memset(c_junk, 0.0)
            ve.tensor_scalar(rstd_o, rstd_o, 1.0 / P, EPS, ALU.mult, ALU.add)
            LD["a7ms"] = b.inc("dve", verify(ve, rstd_o))

        @blk.scalar
        def _(ac):
            b.w(ac, "dve", LD["a7ms"])
            ac.activation(rstd_o, rstd_o, AF.Ln)
            i = ac.activation(rstd_o, rstd_o, AF.Exp, scale=-0.5)
            LD["a7rstd"] = b.inc("act", i)

        @blk.vector
        def _(ve):
            b.w(ve, "act", LD["a7rstd"])
            for h in range(H):
                i = ve.tensor_tensor(o_bf[:, h, :, :], o_bf[:, h, :, :],
                                     rstd_o[:, h * 16:(h + 1) * 16].unsqueeze(2).broadcast_to([P, 16, P]),
                                     ALU.mult)
                b.inc("dve", i)
            LD["a7on"] = b.c["dve"]

        @blk.scalar
        def _(ac):
            b.w(ac, "dve", LD["a7on"])
            b.dma("dSCRO", ac, scr_o.ap().rearrange("(c p) (h v) -> p h c v", p=P, h=H),
                  o_bf[:, :, :, :])
            LD["scro"] = ("dSCRO", b.dma("dSCRO", ac,
                                         scr_o.ap().rearrange("(c p) (h v) -> p h c v", p=P, h=H),
                                         o_bf[:, :, :, :]))

        if "d_o" in DBG:
            @blk.sync
            def _(sy):
                b.wd(sy, *LD["scro"])
                b.dma("dDBG", sy, DBG["d_o"][:, :], scr_o[:, :])

        mark("a7")
        @blk.sync
        def _(sy):
            b.wd(sy, *LD["scro"])
            b.w(sy, "pe", LD["a7_pe"])
            for rep in range(2):
                for h in range(H):
                    c = b.dma("dONRM", sy, onrm_fm[:, h, :],
                              scr_o[:, h * P:(h + 1) * P], transpose=True)
            LD["onrm"] = ("dONRM", c)
            b.wd(sy, *LD["wo_pre"])
            LD["wosb"] = ("dWOS", b.dma("dWOS", sy, wo_sb[:, :, :],
                                        scr_wo.ap().rearrange("(a p) o -> p a o", p=P)))

        @blk.vector
        def _(ve):
            b.wd(ve, *LD["wosb"])
            ve.tensor_reduce(c_junk, wo_sb[:, 7, :], axis=AX.X, op=ALU.add)
            ve.memset(c_junk, 0.0)
            for kc in range(8):
                i = ve.tensor_scalar(wo_sb[:, kc, :], wo_sb[:, kc, :], owc_sb[:, 0:1], None, ALU.mult)
                b.inc("dve", i)
            LD["wofold"] = b.c["dve"]

        a9_act0 = b.c["act"]

        @blk.tensor
        def _(te):
            b.w(te, "dve", LD["wofold"])
            b.wd(te, *LD["onrm"])
            if bank_last[0]:
                b.w(te, bank_last[0][0], bank_last[0][1])
            te.matmul(ps[0][:, :], wo_sb[:, 7, 0:P], onrm_fm[:, 7, 0:512],
                      start=True, stop=True)
            r = 0
            for j in range(32):
                oc, nb = j // 4, j % 4
                if j >= 4:
                    b.w(te, "act", a9_act0 + (j - 4) + 1)
                elif bank_last[r]:
                    b.w(te, bank_last[r][0], bank_last[r][1])
                for hc in range(8):
                    i = te.matmul(ps[r][:, :], wo_sb[:, hc, oc * P:(oc + 1) * P],
                                  onrm_fm[:, hc, nb * 512:(nb + 1) * 512],
                                  start=(hc == 0), stop=(hc == 7))
                b.inc("pe", i)
                LD[f"a9m{j}"] = b.c["pe"]
                r = (r + 1) % 4
        LD["a9_pe"] = b.c["pe"]

        @blk.scalar
        def _(ac):
            r = 0
            aobuf = [mb0[:, 0:512], mb1[:, 0:512], mb0[:, 512:1024], mb1[:, 512:1024]]

            def ao_store(jj):
                oc_, nb_ = jj // 4, jj % 4
                src = aobuf[jj % 4]
                b.dma(f"dAO{jj % 2}", ac,
                      scr_ao[oc_ * P:(oc_ + 1) * P, nb_ * 512:(nb_ + 1) * 512], src)
                LD[f"a9w{jj}"] = (f"dAO{jj % 2}",
                                  b.dma(f"dAO{jj % 2}", ac,
                                        scr_ao[oc_ * P:(oc_ + 1) * P,
                                               nb_ * 512:(nb_ + 1) * 512], src))
            for j in range(32):
                b.w(ac, "pe", LD[f"a9m{j}"])
                if j >= 4:
                    b.wd(ac, *LD[f"a9w{j - 4}"])
                b.inc("act", ac.copy(aobuf[j % 4], ps[r][:, :]))
                if j >= 1:
                    ao_store(j - 1)
                r = (r + 1) % 4
            ac.copy(mf0[:, 0:256], ps[0][:, 0:256])
            ao_store(31)
            LD["a9_act"] = b.c["act"]
        for i in range(4):
            bank_last[i] = ("act", LD["a9_act"])

        if "d_ao" in DBG:
            @blk.sync
            def _(sy):
                b.wd(sy, "dAO0"); b.wd(sy, "dAO1")
                b.dma("dDBG", sy, DBG["d_ao"][:, :], scr_ao[:, :])

        a9_y10 = b.c["dve"]

        @blk.sync
        def _(sy):
            b.wd(sy, "dAO0"); b.wd(sy, "dAO1")
            for t in range(NT):
                b.w(sy, "dve", a9_y10 + (t - 2) + 1 if t >= 2 else 0)
                b.dma(f"dY1L{t % 2}", sy, mf0 if t % 2 == 0 else mf1,
                      x_h[t * P:(t + 1) * P, :])
                b.dma(f"dY1L{t % 2}", sy, flipb[t % 2],
                      scr_ao[:, t * P:(t + 1) * P], transpose=True)
                LD[f"y1l{t}"] = (f"dY1L{t % 2}",
                                 b.dma(f"dY1L{t % 2}", sy, flipb[t % 2],
                                       scr_ao[:, t * P:(t + 1) * P], transpose=True))

        a9_yact0 = b.c["act"]

        @blk.scalar
        def _(ac):
            for t in range(NT):
                b.wd(ac, *LD[f"y1l{t}"])
                mf = mf0 if t % 2 == 0 else mf1
                b.inc("act", ac.activation(mf, mf, AF.Identity,
                                           scale=c_rho1[:, t:t + 1],
                                           bias=c_nmr1[:, t:t + 1]))

        @blk.vector
        def _(ve):
            for t in range(NT):
                b.w(ve, "act", a9_yact0 + t + 1)
                ve.tensor_reduce(c_junk, flipb[t % 2], axis=AX.X, op=ALU.add)
                ve.memset(c_junk, 0.0)
                i = ve.tensor_add(resid[:, t, :], mf0 if t % 2 == 0 else mf1, flipb[t % 2])
                b.inc("dve", i)
            LD["y1"] = b.c["dve"]

        if "d_y1" in DBG:
            @blk.sync
            def _(sy):
                b.w(sy, "dve", LD["y1"])
                b.dma("dDBG", sy, DBG["d_y1"].ap().rearrange("(t p) d -> p t d", p=P),
                      resid[:, :, :])

        mark("a9")
        col_reset()
        c_msq = cols(16)
        c_rh = cols(16)

        @blk.sync
        def _(sy):
            b.w(sy, "pe", LD["a9_pe"])
            b.w(sy, "dve", LD["y1"])
            LD["mnwl"] = ("dMNW", b.dma("dMNW", sy, mf0, mnw_h[:, :]))

        @blk.scalar
        def _(ac):
            b.wd(ac, *LD["mnwl"])
            b.w(ac, "pe", LD["a9_pe"])
            ac.copy(junk_b1, mf0)
            i = ac.copy(mnw_bf, mf0)
            LD["mnwb"] = b.inc("act", i)

        @blk.vector
        def _(ve):
            b.w(ve, "dve", LD["y1"])
            b.w(ve, "pe", LD["a9_pe"])
            for t in range(NT):
                i = ve.scalar_tensor_tensor(junk_b1, resid[:, t, :], 1.0, resid[:, t, :],
                                            ALU.mult, ALU.mult, accum_out=c_msq[:, t:t + 1])
                b.inc("dve", i)
            ve.memset(c_junk, 0.0)
            ve.tensor_scalar(c_rh, c_msq, 1.0 / D, EPS, ALU.mult, ALU.add)
            LD["b1ms"] = b.inc("dve", verify(ve, c_rh))

        b1_hv0 = None

        @blk.scalar
        def _(ac):
            b.w(ac, "dve", LD["b1ms"])
            ac.activation(c_rh, c_rh, AF.Ln)
            i = ac.activation(c_rh, c_rh, AF.Exp, scale=-0.5)
            LD["b1rh"] = b.inc("act", i)

        b1_hv0 = b.c["dve"]
        b1_act1 = b.c["act"]

        @blk.scalar
        def _(ac):
            b.w(ac, "act", LD["b1rh"])
            b.w(ac, "act", LD["mnwb"])
            for t in range(NT):
                if t >= 2:
                    b.w(ac, "dve", b1_hv0 + (t - 2) + 1)
                i = ac.activation(mf0 if t % 2 == 0 else mf1, resid[:, t, :], AF.Identity,
                                  scale=c_rh[:, t:t + 1])
                b.inc("act", i)
                LD[f"b1a{t}"] = b.c["act"]
                if t >= 1:
                    b.w(ac, "dve", b1_hv0 + (t - 1) + 1)
                    LD[f"b1w{t - 1}"] = (f"dH2{(t - 1) % 2}",
                                         b.dma(f"dH2{(t - 1) % 2}", ac, scr_h2[(t - 1) * P:t * P, :],
                                               mb0 if (t - 1) % 2 == 0 else mb1))
            b.w(ac, "dve", b1_hv0 + NT)
            b.dma("dH21", ac, scr_h2[(NT - 1) * P:NT * P, :], mb1)
            LD[f"b1w{NT - 1}"] = ("dH21", b.dma("dH21", ac, scr_h2[(NT - 1) * P:NT * P, :], mb1))

        @blk.vector
        def _(ve):
            for t in range(NT):
                b.w(ve, "act", LD[f"b1a{t}"])
                if t >= 2:
                    b.wd(ve, *LD[f"b1w{t - 2}"])
                i = ve.tensor_mul(mb0 if t % 2 == 0 else mb1, mf0 if t % 2 == 0 else mf1, mnw_bf)
                b.inc("dve", i)

        @blk.sync
        def _(sy):
            b.wd(sy, "dH20"); b.wd(sy, "dH21")
            for rep in range(2):
                for fc in range(8):
                    c = b.dma("dH2F", sy, h2_fm[:, fc, :],
                              scr_h2[:, fc * P:(fc + 1) * P], transpose=True)
            LD["h2fm"] = ("dH2F", c)

        if "d_h2" in DBG:
            @blk.sync
            def _(sy):
                b.wd(sy, "dH20"); b.wd(sy, "dH21")
                b.dma("dDBG", sy, DBG["d_h2"][:, :], scr_h2[:, :])

        b2_pe0 = b.c["pe"]
        b2_act0 = b.c["act"]
        b2_dve0 = b.c["dve"]

        def b2_pe_mg(g, sub, ocl):
            return b2_pe0 + (g * 4 + sub) * 16 + 2 * ocl + 1

        def b2_pe_mu(g, sub, ocl):
            return b2_pe0 + (g * 4 + sub) * 16 + 2 * ocl + 2

        def b2_pe_md(g, sub, outc):
            return b2_pe0 + (g * 4 + sub) * 16 + 8 + outc + 1

        def b2_act_s(j):
            return b2_act0 + j + 1

        def b2_dve_g(j):
            g = j // 16
            sub = (j % 16) // 4
            return b2_dve0 + g * 64 + sub * 12 + (j % 4) + 1

        def b2_dve_d(jj):
            g = jj // 32
            sub = (jj % 32) // 8
            return b2_dve0 + g * 64 + sub * 12 + 4 + (jj % 8) + 1

        def wp2_dve(i5):
            ch = i5 // 8
            return b2_dve0 + ch * 64 + 48 + 2 * (i5 % 8) + 2

        @blk.gpsimd
        def _(gp):
            for g in range(8):
                if g == 0:
                    b.w(gp, "pe", LD["a6_pe"])
                    b.w(gp, "pe", LD["a9_pe"])
                if g >= 2:
                    b.w(gp, "pe", b2_pe0 + (g - 1) * 64)
                b.dma(f"dB2W{g % 2}", gp, wg_view(g % 2),
                      wg_h[:, g * 512:(g + 1) * 512].rearrange("(a p) o -> p a o", p=P))
                b.dma(f"dB2W{g % 2}", gp, wu_view(g % 2),
                      wu_h[:, g * 512:(g + 1) * 512].rearrange("(a p) o -> p a o", p=P))
                LD[f"b2wl{g}"] = (f"dB2W{g % 2}",
                                  b.dma(f"dB2W{g % 2}", gp, wd_view[g % 2],
                                        wd_h[g * 512:(g + 1) * 512, :].rearrange("(a p) o -> p a o", p=P)))

        @blk.tensor
        def _(te):
            b.wd(te, *LD["h2fm"])
            for g in range(8):
                b.wd(te, *LD[f"b2wl{g}"])
                wgv, wuv, wdv = wg_view(g % 2), wu_view(g % 2), wd_view[g % 2]
                if g == 0:
                    if bank_last[0]:
                        b.w(te, bank_last[0][0], bank_last[0][1])
                    te.matmul(ps[0][:, :], wdv[:, 3, 0:P], h2_fm[:, 7, 0:512],
                              start=True, stop=True)
                    te.matmul(ps[0][:, :], h2_fm[:, 7, 0:P], wgv[:, 7, 0:512],
                              start=True, stop=True)
                for sub in range(4):
                    for ocl in range(4):
                        gb, ub = ps[ocl % 2], ps[2 + ocl % 2]
                        j = (g * 4 + sub) * 4 + ocl
                        if j >= 2:
                            b.w(te, "act", b2_act_s(j - 2))
                        elif bank_last[ocl % 2]:
                            b.w(te, bank_last[ocl % 2][0], bank_last[ocl % 2][1])
                        for kc in range(8):
                            i = te.matmul(gb[:, :], wgv[:, kc, ocl * P:(ocl + 1) * P],
                                          h2_fm[:, kc, sub * 512:(sub + 1) * 512],
                                          start=(kc == 0), stop=(kc == 7))
                        b.inc("pe", i)
                        if j >= 2:
                            b.w(te, "dve", b2_dve_g(j - 2))
                        elif bank_last[2 + ocl % 2]:
                            b.w(te, bank_last[2 + ocl % 2][0], bank_last[2 + ocl % 2][1])
                        for kc in range(8):
                            i = te.matmul(ub[:, :], wuv[:, kc, ocl * P:(ocl + 1) * P],
                                          h2_fm[:, kc, sub * 512:(sub + 1) * 512],
                                          start=(kc == 0), stop=(kc == 7))
                        b.inc("pe", i)
                    for outc in range(8):
                        db = ps[4 + outc % 2]
                        jj = (g * 4 + sub) * 8 + outc
                        if jj >= 2:
                            b.w(te, "dve", b2_dve_d(jj - 2))
                        elif bank_last[4 + outc % 2]:
                            b.w(te, bank_last[4 + outc % 2][0], bank_last[4 + outc % 2][1])
                        for ocl in range(4):
                            b.w(te, "dve", b2_dve_g((g * 4 + sub) * 4 + ocl))
                            i = te.matmul(db[:, :], wdv[:, ocl, outc * P:(outc + 1) * P],
                                          g_tile[:, ocl, :], start=(ocl == 0), stop=(ocl == 3))
                        b.inc("pe", i)
        LD["b2_pe"] = b.c["pe"]

        @blk.scalar
        def _(ac):
            for g in range(8):
                for sub in range(4):
                    for ocl in range(4):
                        j = (g * 4 + sub) * 4 + ocl
                        b.w(ac, "pe", b2_pe_mg(g, sub, ocl))
                        if j >= 2:
                            b.w(ac, "dve", b2_dve_g(j - 2))
                        i = ac.activation(sg_bufs[ocl % 2], ps[ocl % 2][:, :], AF.Silu)
                        b.inc("act", i)

        @blk.sync
        def _(sy):
            for i5 in range(64):
                if i5 < 2:
                    b.w(sy, "dve", b1_hv0 + NT)
                    b.wd(sy, "dH20"); b.wd(sy, "dH21")
                else:
                    b.w(sy, "dve", wp2_dve(i5 - 2))
                src = w1_h if i5 < 32 else w2_h
                LD[f"wp2l{i5}"] = (f"dWP2{i5 % 2}",
                                   b.dma(f"dWP2{i5 % 2}", sy, (mf0 if i5 % 2 == 0 else mf1),
                                         wblk(src, i5)))
                if i5 >= 1:
                    b.w(sy, "dve", wp2_dve(i5 - 1))
                    dst_s = scr_w1q if (i5 - 1) < 32 else scr_w2q
                    LD[f"wp2w{i5 - 1}"] = ("dWP2W", b.dma("dWP2W", sy,
                                                          wblk(dst_s, i5 - 1), mb1[:, :]))
            b.w(sy, "dve", wp2_dve(63))
            LD["wp2w63"] = ("dWP2W", b.dma("dWP2W", sy, wblk(scr_w2q, 63), mb1[:, :]))

        for g in range(8):
            def _mk(gg):
                @blk.vector
                def _(ve):
                    for sub in range(4):
                        for ocl in range(4):
                            j = (gg * 4 + sub) * 4 + ocl
                            b.w(ve, "act", b2_act_s(j))
                            b.w(ve, "pe", b2_pe_mu(gg, sub, ocl))
                            i = ve.tensor_mul(g_tile[:, ocl, :], sg_bufs[ocl % 2],
                                              ps[2 + ocl % 2][:, :])
                            b.inc("dve", i)
                        for outc in range(8):
                            jj = (gg * 4 + sub) * 8 + outc
                            b.w(ve, "pe", b2_pe_md(gg, sub, outc))
                            db = ps[4 + outc % 2]
                            if gg == 0:
                                i = ve.tensor_copy(y2acc[:, outc, sub * 512:(sub + 1) * 512],
                                                   db[:, :])
                            else:
                                i = ve.tensor_add(y2acc[:, outc, sub * 512:(sub + 1) * 512],
                                                  y2acc[:, outc, sub * 512:(sub + 1) * 512],
                                                  db[:, :])
                            b.inc("dve", i)

                @blk.vector
                def _(ve):
                    for k5 in range(8):
                        i5 = gg * 8 + k5
                        wi = 0 if i5 < 32 else 1
                        wwb = mf0 if i5 % 2 == 0 else mf1
                        b.wd(ve, *LD[f"wp2l{i5}"])
                        if i5 >= 1:
                            b.wd(ve, *LD[f"wp2w{i5 - 1}"])
                        b.inc("dve", ve.tensor_single_scalar(mb0[:, :], wwb,
                                                             c_nth[:, wi:wi + 1], ALU.is_le))
                        b.inc("dve", ve.scalar_tensor_tensor(mb1[:, :], wwb,
                                                             c_th[:, wi:wi + 1], mb0[:, :],
                                                             ALU.is_ge, ALU.subtract))
            _mk(g)
        LD["b2"] = b.c["dve"]
        assert (not blk.on) or LD["b2"] == b2_dve0 + 512, (LD["b2"], b2_dve0)
        for i in range(4):
            bank_last[i] = ("dve", LD["b2"])
        bank_last[4] = ("dve", LD["b2"])
        bank_last[5] = ("dve", LD["b2"])

        @blk.scalar
        def _(ac):
            b.w(ac, "dve", LD["b2"])
            b.dma("dSCRMO", ac, scr_mo.ap().rearrange("(a p) t -> p a t", p=P), y2acc[:, :, :])
            LD["scrmo"] = ("dSCRMO", b.dma("dSCRMO", ac,
                                           scr_mo.ap().rearrange("(a p) t -> p a t", p=P),
                                           y2acc[:, :, :]))

        if "d_mo" in DBG:
            @blk.sync
            def _(sy):
                b.wd(sy, *LD["scrmo"])
                b.dma("dDBG", sy, DBG["d_mo"][:, :], scr_mo[:, :])

        b3_dve0 = b.c["dve"]

        @blk.sync
        def _(sy):
            b.wd(sy, *LD["scrmo"])
            for t in range(NT):
                b.w(sy, "dve", b3_dve0 + (t - 2) + 1 if t >= 2 else 0)
                b.dma(f"dB3L{t % 2}", sy, flipb[t % 2],
                      scr_mo[:, t * P:(t + 1) * P], transpose=True)
                LD[f"b3l{t}"] = (f"dB3L{t % 2}",
                                 b.dma(f"dB3L{t % 2}", sy, flipb[t % 2],
                                       scr_mo[:, t * P:(t + 1) * P], transpose=True))

        @blk.vector
        def _(ve):
            for t in range(NT):
                b.wd(ve, *LD[f"b3l{t}"])
                ve.tensor_reduce(c_junk, flipb[t % 2], axis=AX.X, op=ALU.add)
                ve.memset(c_junk, 0.0)
                i = ve.tensor_add(resid[:, t, :], resid[:, t, :], flipb[t % 2])
                b.inc("dve", i)
            LD["y2"] = b.c["dve"]

        mark("b3")
        col_reset()
        c_sum4 = cols(16); c_ssq4 = cols(16); c_m4 = cols(16); c_v4 = cols(16)
        c_rh4 = cols(16); c_nm4 = cols(16)

        @blk.vector
        def _(ve):
            b.w(ve, "dve", LD["y2"])
            b.w(ve, "pe", LD["b2_pe"])
            for t in range(NT):
                ve.reduce_sum(c_sum4[:, t:t + 1], resid[:, t, :], axis=AX.X)
                i = ve.scalar_tensor_tensor(S3[:, 5, 0:1024], resid[:, t, :], 1.0,
                                            resid[:, t, :],
                                            ALU.mult, ALU.mult, accum_out=c_ssq4[:, t:t + 1])
                b.inc("dve", i)
            ve.tensor_scalar(c_m4, c_sum4, 1.0 / D, None, ALU.mult)
            ve.tensor_scalar(c_v4, c_ssq4, 1.0 / D, None, ALU.mult)
            ve.tensor_mul(c_rh4, c_m4, c_m4)
            ve.memset(c_junk, 0.0)
            ve.tensor_sub(c_v4, c_v4, c_rh4)
            ve.memset(c_junk, 0.0)
            ve.tensor_scalar(c_v4, c_v4, EPS, None, ALU.add)
            LD["b4v"] = b.inc("dve", verify(ve, c_v4))

        @blk.scalar
        def _(ac):
            b.w(ac, "dve", LD["b4v"])
            ac.activation(c_rh4, c_v4, AF.Ln)
            i = ac.activation(c_rh4, c_rh4, AF.Exp, scale=-0.5)
            LD["b4rh"] = b.inc("act", i)

        @blk.vector
        def _(ve):
            b.w(ve, "act", LD["b4rh"])
            ve.scalar_tensor_tensor(c_nm4, c_m4, -1.0, c_rh4, ALU.mult, ALU.mult)
            LD["b4nm"] = b.inc("dve", verify(ve, c_nm4))

        b4_x0 = b.c["dve"]

        @blk.sync
        def _(sy):
            for t in range(NT):
                b.w(sy, "dve", b4_x0 + (t - 2) + 1 if t >= 2 else 0)
                LD[f"b4l{t}"] = (f"dB4L{t % 2}",
                                 b.dma(f"dB4L{t % 2}", sy, mf0 if t % 2 == 0 else mf1,
                                       x_h[t * P:(t + 1) * P, :]))

        @blk.scalar
        def _(ac):
            b.w(ac, "dve", LD["b4nm"])
            for t in range(NT):
                if t >= 1:
                    b.w(ac, "dve", b4_x0 + (t - 1) + 1)
                i = ac.activation(resid[:, t, :], resid[:, t, :], AF.Identity,
                                  scale=c_rh4[:, t:t + 1], bias=c_nm4[:, t:t + 1])
                b.inc("act", i)
                LD[f"b4a{t}"] = b.c["act"]

        @blk.vector
        def _(ve):
            for t in range(NT):
                b.w(ve, "act", LD[f"b4a{t}"])
                b.wd(ve, *LD[f"b4l{t}"])
                ve.tensor_reduce(c_junk, mf0 if t % 2 == 0 else mf1, axis=AX.X, op=ALU.add)
                ve.memset(c_junk, 0.0)
                i = ve.tensor_add(resid[:, t, :], resid[:, t, :], mf0 if t % 2 == 0 else mf1)
                b.inc("dve", i)
            LD["x1"] = b.c["dve"]

        if "d_x1" in DBG:
            @blk.sync
            def _(sy):
                b.w(sy, "dve", LD["x1"])
                b.dma("dDBG", sy, DBG["d_x1"].ap().rearrange("(t p) d -> p t d", p=P),
                      resid[:, :, :])

        col_reset()
        c_su = cols(16); c_sq = cols(16); c_m1 = cols(16); c_v1 = cols(16)
        c_r1 = cols(16); c_n1 = cols(16); c_mx = cols(16); c_rc = cols(16)
        c_iz = cols(16)

        @blk.vector
        def _(ve):
            b.w(ve, "dve", LD["x1"])
            for t in range(NT):
                ve.reduce_sum(c_su[:, t:t + 1], resid[:, t, :], axis=AX.X)
                i = ve.scalar_tensor_tensor(S3[:, 5, 0:1024], resid[:, t, :], 1.0,
                                            resid[:, t, :],
                                            ALU.mult, ALU.mult, accum_out=c_sq[:, t:t + 1])
                b.inc("dve", i)
            ve.tensor_scalar(c_m1, c_su, 1.0 / D, None, ALU.mult)
            ve.tensor_scalar(c_v1, c_sq, 1.0 / D, None, ALU.mult)
            ve.tensor_mul(c_r1, c_m1, c_m1)
            ve.memset(c_junk, 0.0)
            ve.tensor_sub(c_v1, c_v1, c_r1)
            ve.memset(c_junk, 0.0)
            ve.tensor_scalar(c_v1, c_v1, EPS, None, ALU.add)
            LD["c1v"] = b.inc("dve", verify(ve, c_v1))

        @blk.scalar
        def _(ac):
            b.w(ac, "dve", LD["c1v"])
            ac.activation(c_r1, c_v1, AF.Ln)
            i = ac.activation(c_r1, c_r1, AF.Exp, scale=-0.5)
            LD["c1rh"] = b.inc("act", i)

        @blk.vector
        def _(ve):
            b.w(ve, "act", LD["c1rh"])
            ve.scalar_tensor_tensor(c_n1, c_m1, -1.0, c_r1, ALU.mult, ALU.mult)
            LD["c1nm"] = b.inc("dve", verify(ve, c_n1))

        c1_mx0 = b.c["dve"]

        @blk.scalar
        def _(ac):
            b.w(ac, "dve", LD["c1nm"])
            for t in range(NT):
                b.w(ac, "dve", c1_mx0 + (t - 2) + 1 if t >= 2 else 0)
                i = ac.activation(mf0 if t % 2 == 0 else mf1, resid[:, t, :], AF.Identity,
                                  scale=c_r1[:, t:t + 1], bias=c_n1[:, t:t + 1])
                b.inc("act", i)
                LD[f"c1z{t}"] = b.c["act"]

        @blk.vector
        def _(ve):
            for t in range(NT):
                b.w(ve, "act", LD[f"c1z{t}"])
                i = ve.reduce_max(c_mx[:, t:t + 1], mf0 if t % 2 == 0 else mf1,
                                  axis=AX.X, apply_absolute_value=True)
                b.inc("dve", i)
            ve.memset(c_junk, 0.0)
            i = ve.tensor_scalar(c_mx, c_mx, EPS, None, ALU.max)
            b.inc("dve", i)
            b.w(ve, "dve", b.c["dve"])
            ve.reciprocal(c_rc, c_mx)
            ve.tensor_scalar(c_iz, c_mx, 1.0 / 127.0, None, ALU.mult)
            LD["c1sc"] = b.inc("dve", verify(ve, c_rc, c_iz))

        c1_q0 = b.c["dve"]

        zst16 = [S3[:, q // 2, (q % 2) * 1024:(q % 2) * 1024 + 1024] for q in range(16)]

        @blk.scalar
        def _(ac):
            b.w(ac, "dve", LD["c1sc"])
            for t in range(NT):
                if t >= 1:
                    tp = t - 1
                    b.w(ac, "dve", c1_q0 + tp + 1)
                    b.inc("act", ac.copy(mf1.bitcast(I32), mf1))
                    b.inc("act", ac.copy(zst16[tp], mf1.bitcast(I32)))
                    LD[f"c1c{tp}"] = b.c["act"]
                i = ac.activation(mf0, resid[:, t, :], AF.Identity,
                                  scale=c_r1[:, t:t + 1], bias=c_n1[:, t:t + 1])
                b.inc("act", i)
                LD[f"c1za{t}"] = b.c["act"]
            tp = NT - 1
            b.w(ac, "dve", c1_q0 + tp + 1)
            b.inc("act", ac.copy(mf1.bitcast(I32), mf1))
            b.inc("act", ac.copy(zst16[tp], mf1.bitcast(I32)))
            LD[f"c1c{tp}"] = b.c["act"]
            i = ac.copy(mb0, mf0)
            LD["c1tail"] = b.inc("act", i)

        @blk.vector
        def _(ve):
            for t in range(NT):
                b.w(ve, "act", LD[f"c1za{t}"])
                if t >= 1:
                    b.w(ve, "act", LD[f"c1c{t - 1}"])
                i = ve.tensor_scalar(mf1, mf0, c_rc[:, t:t + 1], 127.0, ALU.mult, ALU.mult)
                b.inc("dve", i)
                LD[f"c1s{t}"] = b.c["dve"]

        mark("c1")
        @blk.tensor
        def _(te):
            b.w(te, "dve", LD["c1sc"])
            b.w(te, "dve", LD["b2"])
            b.w(te, bank_last[5][0], bank_last[5][1])
            i = te.transpose(ps[5][0:16, 0:P], c_iz, idf_sb[:, :])
            b.inc("pe", i)
            LD["m1t"] = b.c["pe"]

        @blk.vector
        def _(ve):
            b.w(ve, "pe", LD["m1t"])
            ve.tensor_copy(trow[0:16, 0:P], ps[5][0:16, 0:P])
            LD["m1r"] = b.inc("dve", verify(ve, trow[0:16, 0:P]))

        @blk.sync
        def _(sy):
            b.w(sy, "dve", LD["m1r"])
            LD["m1row"] = ("dM1R", b.dma("dM1R", sy,
                                         mrow_f[0:1, 0:N].rearrange("p (a d) -> p a d", d=P),
                                         trow[0:16, 0:P]))

        @blk.vector
        def _(ve):
            b.wd(ve, *LD["m1row"])
            ve.tensor_reduce(c_junk[0:1, :], mrow_f[0:1, 0:N], axis=AX.X, op=ALU.add)
            ve.memset(c_junk, 0.0)
            i = ve.tensor_scalar(mrow_f[0:1, 0:N], mrow_f[0:1, 0:N], c_s[0:1, 0:1], None, ALU.mult)
            b.inc("dve", i)
            b.w(ve, "dve", b.c["dve"])
            ve.tensor_copy(mrow_b[0:1, 0:N], mrow_f[0:1, 0:N])
            LD["m1rb"] = b.inc("dve", verify(ve, mrow_b[0:1, 0:N]))

        m1_e0 = b.c["act"]

        @blk.tensor
        def _(te):
            b.w(te, "dve", LD["m1rb"])
            b.w(te, bank_last[6][0], bank_last[6][1])
            for nb in range(4):
                if nb >= 1:
                    b.w(te, "act", m1_e0 + nb)
                i = te.matmul(ps[6][:, :], onesr_sb[0:1, :], mrow_b[0:1, nb * 512:(nb + 1) * 512],
                              start=True, stop=True)
                b.inc("pe", i)
                LD[f"m1b{nb}"] = b.c["pe"]

        @blk.scalar
        def _(ac):
            for nb in range(4):
                b.w(ac, "pe", LD[f"m1b{nb}"])
                b.inc("act", ac.copy(m1_sb[:, nb * 512:(nb + 1) * 512], ps[6][:, :]))
            LD["m1"] = b.c["act"]

        if "d_z" in DBG:
            @blk.sync
            def _(sy):
                b.w(sy, "act", LD["c1tail"])
                b.dma("dDBG", sy, DBG["d_z"].ap().rearrange("(a p) d -> p a d", p=P),
                      S3[:, :, :].rearrange("p a (q d) -> p (a q) d", d=1024))
        if "d_w1q" in DBG:
            @blk.sync
            def _(sy):
                b.wd(sy, "dWP2W")
                b.dma("dDBG", sy, DBG["d_w1q"][:, :], scr_w1q[:, :])
                b.dma("dDBG", sy, DBG["d_w2q"][:, :], scr_w2q[:, :])

        @blk.sync
        def _(sy):
            b.w(sy, "pe", LD["b2_pe"])
            for t in range(NT):
                b.w(sy, "act", LD[f"c1za{t + 1}"] if t + 1 < NT else LD["c1tail"])
                for fc in range(8):
                    c = b.dma("dZF", sy, zint_fm[:, fc, t * P:(t + 1) * P],
                              zst16[t][:, fc * P:(fc + 1) * P], transpose=True)
            LD["zfm"] = ("dZF", c)

        @blk.vector
        def _(ve):
            b.w(ve, "dve", LD["b2"])
            b.inc("dve", ve.memset(gmax, 0.0))
            i = ve.memset(gmin, 0.0)
            LD["gmax0"] = b.inc("dve", i)

        _c2pe0 = b.c["pe"]

        @blk.sync
        def _(sy):
            b.w(sy, "pe", LD["b2_pe"])
            b.wd(sy, "dWP2W")
            for ho in range(2):
                if ho == 1:
                    b.w(sy, "pe", _c2pe0 + 64)
                for ic in range(8):
                    c = b.dma(f"dW1T{ho}", sy, w1T[:, ic, :],
                              scr_w1q[ho * N:(ho + 1) * N, ic * P:(ic + 1) * P], transpose=True)
                LD[f"w1T{ho}"] = (f"dW1T{ho}", c)

        @blk.vector
        def _(ve):
            b.wd(ve, *LD["zfm"])
            b.wd(ve, *LD["w1T0"])
            ve.tensor_reduce(c_junk, zint_fm[:, 7, :], axis=AX.X, op=ALU.add)
            ve.memset(c_junk, 0.0)
            i = ve.tensor_reduce(c_junk, w1T[:, 7, :], axis=AX.X, op=ALU.add)
            LD["c2set"] = b.inc("dve", i)

        c2_pe0 = b.c["pe"]
        assert (not blk.on) or c2_pe0 == _c2pe0
        c2_dve0 = b.c["dve"]
        c2_act0 = b.c["act"]

        @blk.tensor
        def _(te):
            b.w(te, "dve", LD["c2set"])
            b.w(te, "act", LD["m1"])
            b.w(te, "dve", LD["gmax0"])
            for ho in range(2):
                b.wd(te, *LD[f"w1T{ho}"])
                if ho == 1:
                    b.w(te, "dve", c2_dve0 + 15 * 6 + 1)
                    te.matmul(ps[0][:, :], w1T[:, 7, 0:P], zint_fm[:, 7, 0:512],
                              start=True, stop=True)
                for oc in range(16):
                    occ = ho * 16 + oc
                    for nb in range(4):
                        j = occ * 4 + nb
                        r = j % 4
                        if j >= 4:
                            b.w(te, "dve", c2_dve0 + ((j - 4) // 4) * 6 + ((j - 4) % 4) + 1)
                        elif bank_last[r]:
                            b.w(te, bank_last[r][0], bank_last[r][1])
                        for ic in range(8):
                            i = te.matmul(ps[r][:, :], w1T[:, ic, oc * P:(oc + 1) * P],
                                          zint_fm[:, ic, nb * 512:(nb + 1) * 512],
                                          start=(ic == 0), stop=(ic == 7))
                        b.inc("pe", i)
        LD["c2_pe"] = b.c["pe"]

        @blk.vector
        def _(ve):
            for occ in range(32):
                oce = occ % 2
                for nb in range(4):
                    j = occ * 4 + nb
                    b.w(ve, "pe", c2_pe0 + j + 1)
                    if occ >= 2 and nb == 0:
                        b.wd(ve, f"dGW{occ % 2}", 16 * ((occ - 2) // 2 + 1))
                    i = ve.tensor_tensor(g2row[oce][:, nb * 512:(nb + 1) * 512],
                                         ps[j % 4][:, :], m1_sb[:, nb * 512:(nb + 1) * 512],
                                         ALU.mult)
                    b.inc("dve", i)
                b.w(ve, "act", c2_act0 + occ * 4 + 4)
                b.inc("dve", ve.tensor_max(gmax, gmax, g2row[oce][:, :]))
                i = ve.tensor_tensor(gmin, gmin, g2row[oce][:, :], ALU.min)
                b.inc("dve", i)
        LD["c2gm"] = b.c["dve"]

        @blk.scalar
        def _(ac):
            for occ in range(32):
                oce = occ % 2
                for nb in range(4):
                    j = occ * 4 + nb
                    b.w(ac, "dve", c2_dve0 + occ * 6 + nb + 1)
                    i = ac.activation(g2row[oce][:, nb * 512:(nb + 1) * 512],
                                      g2row[oce][:, nb * 512:(nb + 1) * 512], AF.Gelu)
                    b.inc("act", i)
                b.w(ac, "dve", c2_dve0 + occ * 6 + 6)
                ac.copy(mf0, g2row[oce][:, 0:1024])
                LD[f"c2w{occ}"] = (f"dGW{occ % 2}",
                                   b.dma(f"dGW{occ % 2}", ac, scr_g[occ * P:(occ + 1) * P, :],
                                         g2row[oce][:, :]))
        for i in range(4):
            bank_last[i] = ("dve", LD["c2gm"])

        if "d_g" in DBG:
            @blk.sync
            def _(sy):
                b.wd(sy, "dGW0"); b.wd(sy, "dGW1")
                b.dma("dDBG", sy, DBG["d_g"][:, :], scr_g[:, :])

        mark("c2")
        col_reset()
        c_gmx = cols(16)
        c_gsc = cols(16)

        @blk.vector
        def _(ve):
            b.w(ve, "dve", LD["c2gm"])
            b.inc("dve", ve.tensor_scalar(gmin, gmin, -1.0, None, ALU.mult))
            ve.tensor_max(gmax, gmax, gmin)
            LD["c2gm2"] = b.inc("dve", verify(ve, gmax))
        c3_rd0 = b.c["dve"]

        @blk.tensor
        def _(te):
            b.w(te, "dve", LD["c2gm2"])
            b.w(te, bank_last[4][0], bank_last[4][1])
            b.w(te, bank_last[5][0], bank_last[5][1])
            for t in range(NT):
                r5 = ps[4 + t % 2]
                b.w(te, "dve", c3_rd0 + (t - 2) + 1 if t >= 2 else 0)
                i = te.transpose(r5[0:P, 0:P], gmax[:, t * P:(t + 1) * P], idf_sb[:, :])
                b.inc("pe", i)
                LD[f"c3t{t}"] = b.c["pe"]

        @blk.vector
        def _(ve):
            for t in range(NT):
                b.w(ve, "pe", LD[f"c3t{t}"])
                i = ve.reduce_max(c_gmx[:, t:t + 1], ps[4 + t % 2][:, 0:P], axis=AX.X)
                b.inc("dve", i)
            ve.memset(c_junk, 0.0)
            i = ve.tensor_scalar(c_gmx, c_gmx, EPS, None, ALU.max)
            b.inc("dve", i)
            b.w(ve, "dve", b.c["dve"])
            ve.reciprocal(c_gsc, c_gmx)
            LD["c3sc"] = b.inc("dve", verify(ve, c_gsc))

        @blk.tensor
        def _(te):
            b.w(te, "dve", LD["c3sc"])
            i = te.transpose(ps[5][0:16, 0:P], c_gsc, idf_sb[:, :])
            b.inc("pe", i)
            LD["m2t"] = b.c["pe"]

        @blk.vector
        def _(ve):
            b.w(ve, "pe", LD["m2t"])
            ve.tensor_copy(trow[0:16, 0:P], ps[5][0:16, 0:P])
            LD["m2r"] = b.inc("dve", verify(ve, trow[0:16, 0:P]))

        @blk.sync
        def _(sy):
            b.w(sy, "dve", LD["m2r"])
            LD["m2row"] = ("dM1R", b.dma("dM1R", sy,
                                         mrow_f[0:1, 0:N].rearrange("p (a d) -> p a d", d=P),
                                         trow[0:16, 0:P]))

        @blk.vector
        def _(ve):
            b.wd(ve, *LD["m2row"])
            ve.tensor_reduce(c_junk[0:1, :], mrow_f[0:1, 0:N], axis=AX.X, op=ALU.add)
            ve.memset(c_junk, 0.0)
            i = ve.tensor_scalar(mrow_f[0:1, 0:N], mrow_f[0:1, 0:N], 127.0, None, ALU.mult)
            b.inc("dve", i)
            b.w(ve, "dve", b.c["dve"])
            ve.tensor_copy(mrow_b[0:1, 0:N], mrow_f[0:1, 0:N])
            LD["m2rb"] = b.inc("dve", verify(ve, mrow_b[0:1, 0:N]))

        m2_e0 = b.c["act"]

        @blk.tensor
        def _(te):
            b.w(te, "dve", LD["m2rb"])
            b.w(te, bank_last[6][0], bank_last[6][1])
            for nb in range(4):
                if nb >= 1:
                    b.w(te, "act", m2_e0 + nb)
                i = te.matmul(ps[6][:, :], onesr_sb[0:1, :], mrow_b[0:1, nb * 512:(nb + 1) * 512],
                              start=True, stop=True)
                b.inc("pe", i)
                LD[f"m2b{nb}"] = b.c["pe"]

        @blk.scalar
        def _(ac):
            for nb in range(4):
                b.w(ac, "pe", LD[f"m2b{nb}"])
                b.inc("act", ac.copy(m2_sb[:, nb * 512:(nb + 1) * 512], ps[6][:, :]))
            LD["m2"] = b.c["act"]

        c4_dve0 = b.c["dve"]
        c4_act0 = b.c["act"]
        c4_pe0 = b.c["pe"]

        def c4_act_c(j):
            return c4_act0 + (j // 32) * 72 + 2 * (j % 32) + 2

        @blk.sync
        def _(sy):
            b.w(sy, "act", LD["m2"])
            b.wd(sy, "dGW0")
            b.wd(sy, "dGW1")
            for nb in range(4):
                for ic in range(32):
                    j = nb * 32 + ic
                    if j >= 4:
                        b.w(sy, "dve", c4_dve0 + (j - 4) + 1)
                        b.w(sy, "pe", c4_pe0 + (j - 4) + 1)
                    b.dma(f"dC4L{ic % 4}", sy, g2rb[ic % 4],
                          scr_g[ic * P:(ic + 1) * P, nb * 512:(nb + 1) * 512])
                    LD[f"c4l{j}"] = (f"dC4L{ic % 4}",
                                     b.dma(f"dC4L{ic % 4}", sy, w2T_buf[ic % 4],
                                           scr_w2q[:, ic * P:(ic + 1) * P], transpose=True))

        @blk.vector
        def _(ve):
            for nb in range(4):
                for ic in range(32):
                    j = nb * 32 + ic
                    b.wd(ve, *LD[f"c4l{j}"])
                    b.w(ve, "act", c4_act_c(j - 1) if j >= 1 else 0)
                    i = ve.tensor_tensor(mf1[:, 0:512], g2rb[ic % 4],
                                         m2_sb[:, nb * 512:(nb + 1) * 512], ALU.mult)
                    b.inc("dve", i)
                    LD[f"c4z{j}"] = b.c["dve"]

        @blk.scalar
        def _(ac):
            for nb in range(4):
                for ic in range(32):
                    j = nb * 32 + ic
                    b.w(ac, "dve", LD[f"c4z{j}"])
                    if j >= 32:
                        b.w(ac, "pe", c4_pe0 + (j - 32) + 1)
                    b.inc("act", ac.copy(mf1.bitcast(I32)[:, 0:512], mf1[:, 0:512]))
                    b.inc("act", ac.copy(gq_sb[:, ic, :], mf1.bitcast(I32)[:, 0:512]))
                b.w(ac, "pe", c4_pe0 + 32 * (nb + 1))
                for outc in range(8):
                    b.inc("act", ac.copy(f_fm[:, outc, nb * 512:(nb + 1) * 512], ps[outc][:, :]))
            LD["c4_act"] = b.c["act"]

        @blk.tensor
        def _(te):
            for nb in range(4):
                for ic in range(32):
                    j = nb * 32 + ic
                    b.w(te, "act", c4_act_c(j))
                    if ic == 0:
                        if nb == 0:
                            for bb in range(8):
                                if bank_last[bb]:
                                    b.w(te, bank_last[bb][0], bank_last[bb][1])
                        else:
                            b.w(te, "act", c4_act0 + nb * 72)
                    for outc in range(8):
                        i = te.matmul(ps[outc][:, :], w2T_buf[ic % 4][:, outc * P:(outc + 1) * P],
                                      gq_sb[:, ic, :], start=(ic == 0), stop=(ic == 31))
                    b.inc("pe", i)
                LD[f"c4pe{nb}"] = b.c["pe"]

        mark("c4")
        @blk.scalar
        def _(ac):
            b.w(ac, "act", LD["c4_act"])
            b.dma("dSCRF", ac, scr_f.ap().rearrange("(a p) t -> p a t", p=P), f_fm[:, :, :])
            LD["scrf"] = ("dSCRF", b.dma("dSCRF", ac,
                                         scr_f.ap().rearrange("(a p) t -> p a t", p=P),
                                         f_fm[:, :, :]))

        if "d_f" in DBG:
            @blk.sync
            def _(sy):
                b.wd(sy, *LD["scrf"])
                b.dma("dDBG", sy, DBG["d_f"][:, :], scr_f[:, :])

        @blk.sync
        def _(sy):
            b.wd(sy, *LD["scrf"])
            for rep in range(2):
                for t in range(NT):
                    c = b.dma("dFF", sy, f_tm[:, t, :],
                              scr_f[:, t * P:(t + 1) * P], transpose=True)
            LD["ffm"] = ("dFF", c)

        col_reset()
        c_su5 = cols(16); c_sq5 = cols(16); c_m5 = cols(16); c_v5 = cols(16)
        c_r5 = cols(16); c_n5 = cols(16)

        @blk.vector
        def _(ve):
            b.wd(ve, *LD["ffm"])
            for t in range(NT):
                ve.reduce_sum(c_su5[:, t:t + 1], f_tm[:, t, :], axis=AX.X)
                i = ve.scalar_tensor_tensor(junk_c, f_tm[:, t, :], 1.0, f_tm[:, t, :],
                                            ALU.mult, ALU.mult, accum_out=c_sq5[:, t:t + 1])
                b.inc("dve", i)
            ve.tensor_scalar(c_m5, c_su5, 1.0 / D, None, ALU.mult)
            ve.tensor_scalar(c_v5, c_sq5, 1.0 / D, None, ALU.mult)
            ve.tensor_mul(c_r5, c_m5, c_m5)
            ve.memset(c_junk, 0.0)
            ve.tensor_sub(c_v5, c_v5, c_r5)
            ve.memset(c_junk, 0.0)
            ve.tensor_scalar(c_v5, c_v5, EPS, None, ALU.add)
            LD["c5v"] = b.inc("dve", verify(ve, c_v5))

        @blk.scalar
        def _(ac):
            b.w(ac, "dve", LD["c5v"])
            ac.activation(c_r5, c_v5, AF.Ln)
            i = ac.activation(c_r5, c_r5, AF.Exp, scale=-0.5)
            LD["c5rh"] = b.inc("act", i)

        @blk.vector
        def _(ve):
            b.w(ve, "act", LD["c5rh"])
            ve.scalar_tensor_tensor(c_n5, c_m5, -1.0, c_r5, ALU.mult, ALU.mult)
            LD["c5nm"] = b.inc("dve", verify(ve, c_n5))

        c5_x0 = b.c["dve"]

        @blk.scalar
        def _(ac):
            b.w(ac, "dve", LD["c5nm"])
            for t in range(NT):
                b.w(ac, "dve", c5_x0 + (t - 2) + 1 if t >= 2 else 0)
                i = ac.activation(mf0 if t % 2 == 0 else mf1, f_tm[:, t, :], AF.Identity,
                                  scale=c_r5[:, t:t + 1], bias=c_n5[:, t:t + 1])
                b.inc("act", i)
                LD[f"c5a{t}"] = b.c["act"]
                if t >= 2:
                    b.w(ac, "dve", c5_x0 + t)
                    b.dma("dOUT", ac, out_h[(t - 2) * P:(t - 1) * P, :], resid[:, t - 2, :])
            b.w(ac, "dve", c5_x0 + NT)
            b.dma("dOUT", ac, out_h[(NT - 2) * P:(NT - 1) * P, :], resid[:, NT - 2, :])
            b.w(ac, "dve", c5_x0 + NT + 1)
            b.dma("dOUT", ac, out_h[(NT - 1) * P:NT * P, :], resid[:, NT - 1, :])

        @blk.vector
        def _(ve):
            for t in range(NT):
                b.w(ve, "act", LD[f"c5a{t}"])
                i = ve.tensor_add(resid[:, t, :], resid[:, t, :], mf0 if t % 2 == 0 else mf1)
                b.inc("dve", i)
            LD["c5fin"] = b.inc("dve", verify(ve, resid[:, NT - 1, :]))

        @blk.sync
        def _(sy):
            b.wd(sy, "dOUT")
            if DBG:
                b.wd(sy, "dDBG")

    return nc


def _prep_inputs(inputs):
    x = np.asarray(inputs["L0_attn_hidden_states"], dtype=np.float32).reshape(16384, D)
    base = dict(
        wq=np.ascontiguousarray(np.asarray(inputs["L0_attn_Wq"], dtype=np.float32)),
        wk=np.ascontiguousarray(np.asarray(inputs["L0_attn_Wk"], dtype=np.float32)),
        wv=np.ascontiguousarray(np.asarray(inputs["L0_attn_Wv"], dtype=np.float32)),
        wo=np.ascontiguousarray(np.asarray(inputs["L0_attn_Wo"], dtype=np.float32)),
        wg=np.ascontiguousarray(np.asarray(inputs["L0_attn_Wgate"], dtype=np.float32)),
        wu=np.ascontiguousarray(np.asarray(inputs["L0_attn_Wup"], dtype=np.float32)),
        wd=np.ascontiguousarray(np.asarray(inputs["L0_attn_Wdown"], dtype=np.float32)),
        w1=np.ascontiguousarray(np.asarray(inputs["L0_ffn_w1"], dtype=np.float32)),
        w2=np.ascontiguousarray(np.asarray(inputs["L0_ffn_w2"], dtype=np.float32)),
        anw=np.ascontiguousarray(np.broadcast_to(
            np.asarray(inputs["L0_attn_attn_norm_w"], dtype=np.float32), (P, D))),
        mnw=np.ascontiguousarray(np.broadcast_to(
            np.asarray(inputs["L0_attn_mlp_norm_w"], dtype=np.float32), (P, D))),
        owc=np.ascontiguousarray(
            np.asarray(inputs["L0_attn_attn_out_norm_w"], dtype=np.float32).reshape(P, 1)),
        mask=np.ascontiguousarray(np.triu(np.ones((P, P), np.float32)).astype(ml_dtypes.bfloat16)),
        idf=np.ascontiguousarray(np.eye(P, dtype=np.float32)),
        onesr=np.ones((1, P), np.float32).astype(ml_dtypes.bfloat16),
        onesf=np.ones((P, P), np.float32),
    )
    in_maps = []
    for c in range(NCORES):
        m = dict(base)
        m["x"] = np.ascontiguousarray(x[c * N:(c + 1) * N, :])
        m["mcol"] = np.ascontiguousarray(
            np.broadcast_to((np.arange(NCORES) < c).astype(np.float32), (P, NCORES)))
        in_maps.append(m)
    return in_maps


def kernel(**inputs):
    in_maps = _prep_inputs(inputs)
    nc = build()
    res = run_bass_kernel_spmd(nc, in_maps, core_ids=list(range(NCORES)))
    out = np.concatenate([res.results[c]["out"] for c in range(NCORES)], axis=0)
    return out.reshape(1, 16384, D).astype(np.float32)
